# version 20
# baseline (speedup 1.0000x reference)
import contextlib

import numpy as np
import ml_dtypes
import concourse.bass as bass
import concourse.mybir as mybir
from concourse.bass_utils import run_bass_kernel_spmd

F32 = mybir.dt.float32
BF16 = mybir.dt.bfloat16
I8 = mybir.dt.int8
AF = mybir.ActivationFunctionType
OP = mybir.AluOpType
AX = mybir.AxisListType

T = 4096
D = 1024
NST = 16
RNK = 64
E = 96
TC = 512
NCH = T // TC
NT = TC // 128
NDT = D // 128
NOC = 8


class Ctr:
    def __init__(self, step=1):
        self.n, self.step = 0, step

    def inc(self):
        self.n += self.step
        return self.n


def build_nc():
    nc = bass.Bass("TRN2", target_bir_lowering=False, detect_race_conditions=False)

    x_h = nc.dram_tensor("x", [T, D], F32, kind="ExternalInput").ap()
    xpw_h = nc.dram_tensor("xpw", [D, E], F32, kind="ExternalInput").ap()
    dtw_h = nc.dram_tensor("dtw", [RNK, D], F32, kind="ExternalInput").ap()
    dtb_h = nc.dram_tensor("dtb", [128, NDT], F32, kind="ExternalInput").ap()
    alog_h = nc.dram_tensor("alog", [128, 128], F32, kind="ExternalInput").ap()
    dvec_h = nc.dram_tensor("dvec", [128, NDT], F32, kind="ExternalInput").ap()
    wqt_h = nc.dram_tensor("wqt", [128, NDT, D], I8, kind="ExternalInput").ap()
    wsc_h = nc.dram_tensor("wsc", [128, NOC], F32, kind="ExternalInput").ap()
    biasr_h = nc.dram_tensor("biasr", [128, D], F32, kind="ExternalInput").ap()
    ident_h = nc.dram_tensor("ident", [128, 128], F32, kind="ExternalInput").ap()
    ones_h = nc.dram_tensor("onesm", [128, 128], F32, kind="ExternalInput").ap()
    sel8_h = nc.dram_tensor("sel8", [128, 8, 128], F32, kind="ExternalInput").ap()
    out_h = nc.dram_tensor("out", [T, D], F32, kind="ExternalOutput").ap()
    bc_d = nc.dram_tensor("bc_dram", [NCH, 2 * NST, TC], F32, kind="Internal").ap()

    st = contextlib.ExitStack()
    sb = lambda name, shape, dt=F32: st.enter_context(nc.sbuf_tensor(name, shape, dt))
    psmk = lambda name, shape, dt=F32: st.enter_context(nc.psum_tensor(name, shape, dt))
    mksem = lambda name: st.enter_context(nc.semaphore(name))

    with st:
        s_xpw = sb("s_xpw", [128, NDT, E])
        s_dtw = sb("s_dtw", [128, D], BF16)
        s_wq = sb("s_wq", [128, NDT, D], BF16)
        s_A = sb("s_A", [128, 128])
        s_Ae = sb("s_Ae", [128, 128])
        s_dtb = sb("s_dtb", [128, NDT])
        s_D = sb("s_D", [128, NDT])
        s_ws = sb("s_ws", [128, NOC])
        s_biasrep = sb("s_biasrep", [128, D])
        s_id = sb("s_id", [128, 128])
        s_onesm = sb("s_onesm", [128, 128])
        s_carry = sb("s_carry", [128, 2, 128])
        s_stat = sb("s_stat", [128, 16])
        s_xs = sb("s_xs", [128, NT])
        s_inv = sb("s_inv", [128, NT])
        s_row = sb("s_row", [128, 2, TC])
        s_sel8 = sb("s_sel8", [128, 8, 128])
        s_eps = sb("s_eps", [128, 1])
        s_one1 = sb("s_one1", [128, 1])
        s_dA = sb("s_dA", [128, 2, 8, TC])
        s_yq = sb("s_yq", [128, 1, TC])
        s_drank = sb("s_drank", [128, TC], BF16)
        bA1 = sb("bA1", [128, 2, 1024])
        bA2 = sb("bA2", [128, 2, 1024])
        bB = sb("bB", [128, NDT, TC])
        bC = sb("bC", [128, NDT, TC])
        bD = sb("bD", [128, NDT, TC])
        bE = sb("bE", [128, 8, TC])
        bF = sb("bF", [128, 8, TC])
        bH = sb("bH", [128, 8, TC])
        bJ = sb("bJ", [128, 8, TC])
        bL = sb("bL", [128, NDT, TC], BF16)
        psA = psmk("psA", [128, TC])
        psB = psmk("psB", [128, TC])
        psC = psmk("psC", [128, TC])
        pt0 = psmk("pt0", [128, TC])
        pt1 = psmk("pt1", [128, TC])
        pt2 = psmk("pt2", [128, TC])
        pt3 = psmk("pt3", [128, TC])

        bA1f = bA1.ap().rearrange("p a b -> p (a b)")
        bA2f = bA2.ap().rearrange("p a b -> p (a b)")
        bEf = bE.ap().rearrange("p a b -> p (a b)")
        bFf = bF.ap().rearrange("p a b -> p (a b)")
        bHf = bH.ap().rearrange("p a b -> p (a b)")
        bBf = bB.ap().rearrange("p a b -> p (a b)")
        bCf = bC.ap().rearrange("p a b -> p (a b)")
        bDf = bD.ap().rearrange("p a b -> p (a b)")
        bJf = bJ.ap().rearrange("p a b -> p (a b)")

        bar = mksem("bar")
        m_v = mksem("m_v")
        m_a = mksem("m_a")
        m_p = mksem("m_p")
        m_g = mksem("m_g")
        m_o0 = mksem("m_o0")
        m_o1 = mksem("m_o1")
        m_s0 = mksem("m_s0")
        m_s1 = mksem("m_s1")
        m_r0 = mksem("m_r0")
        m_r1 = mksem("m_r1")
        m_q = mksem("m_q")

        def emit(en, eng):
            on = lambda e: en == e
            nbar = [0]
            c_v, c_a, c_p, c_q = Ctr(), Ctr(), Ctr(), Ctr()
            c_g = Ctr(16)
            c_o = [Ctr(16), Ctr(16)]
            m_o = [m_o0, m_o1]
            c_sx = [Ctr(16), Ctr(16)]
            m_sx = [m_s0, m_s1]
            c_r = [Ctr(16), Ctr(16)]
            m_r = [m_r0, m_r1]
            pe_mode = [None]

            def V(fn, *waits):
                val = c_v.inc()
                if on("vector"):
                    for s, v in waits:
                        if v > 0:
                            eng.wait_ge(s, v)
                    fn(eng).then_inc(m_v, 1)
                return val

            def A(fn, *waits):
                val = c_a.inc()
                if on("scalar"):
                    for s, v in waits:
                        if v > 0:
                            eng.wait_ge(s, v)
                    fn(eng).then_inc(m_a, 1)
                return val

            def Q(fn, *waits):
                prev = c_q.n
                val = c_q.inc()
                if on("gpsimd"):
                    for s, v in waits:
                        if v > 0:
                            eng.wait_ge(s, v)
                    if prev > 0:
                        eng.wait_ge(m_q, prev)
                    fn(eng).then_inc(m_q, 1)
                return val

            def P(fn, mode, *waits):
                sw = pe_mode[0] is not None and pe_mode[0] != mode
                pe_mode[0] = mode
                val = c_p.inc()
                if on("tensor"):
                    if sw:
                        eng.drain()
                    for s, v in waits:
                        if v > 0:
                            eng.wait_ge(s, v)
                    fn(eng).then_inc(m_p, 1)
                return val

            def barrier():
                nbar[0] += 1
                if en == "vector":
                    eng.wait_ge(m_v, c_v.n)
                elif en == "scalar":
                    eng.wait_ge(m_a, c_a.n)
                elif en == "tensor":
                    eng.wait_ge(m_p, c_p.n)
                elif en == "sync":
                    eng.wait_ge(m_s0, c_sx[0].n)
                    eng.wait_ge(m_s1, c_sx[1].n)
                    eng.wait_ge(m_r0, c_r[0].n)
                    eng.wait_ge(m_r1, c_r[1].n)
                else:
                    eng.wait_ge(m_g, c_g.n)
                    eng.wait_ge(m_q, c_q.n)
                    eng.wait_ge(m_o0, c_o[0].n)
                    eng.wait_ge(m_o1, c_o[1].n)
                eng.sem_inc(bar, 1)
                eng.wait_ge(bar, 5 * nbar[0])

            def wait_all_out(e):
                e.wait_ge(m_o0, c_o[0].n)
                e.wait_ge(m_o1, c_o[1].n)

            for tgt, src in ((s_xpw[:], xpw_h.rearrange("(c p) e -> p c e", p=128)),
                             (s_dtw[0:RNK, :], dtw_h), (s_wq[:], wqt_h)):
                c_g.inc()
                if on("gpsimd"):
                    eng.dma_start(out=tgt, in_=src).then_inc(m_g, 16)
            for tgt, src in ((s_Ae[:], alog_h), (s_dtb[:], dtb_h), (s_D[:], dvec_h),
                             (s_ws[:], wsc_h), (s_biasrep[:], biasr_h),
                             (s_id[:], ident_h), (s_onesm[:], ones_h),
                             (s_sel8[:], sel8_h)):
                c_sx[0].inc()
                if on("sync"):
                    eng.dma_start(out=tgt, in_=src).then_inc(m_s0, 16)
            if on("gpsimd"):
                eng.wait_ge(m_g, c_g.n)
            if on("sync"):
                eng.wait_ge(m_s0, c_sx[0].n)
            barrier()
            v_aexp = A(lambda e: e.activation(out=s_A[:], in_=s_Ae[:], func=AF.Exp))
            V(lambda e: e.tensor_scalar(out=s_A[:], in0=s_A[:], scalar1=-1.0,
                                        scalar2=None, op0=OP.mult), (m_a, v_aexp))
            V(lambda e: e.memset(s_carry.ap()[:, 0, :], 0.0))
            V(lambda e: e.memset(s_eps[:], 1e-5))
            V(lambda e: e.memset(s_dtw[RNK:128, :], 0.0))
            V(lambda e: e.memset(s_drank[RNK:128, :], 0.0))
            V(lambda e: e.memset(s_row.ap().rearrange("p a b -> p (a b)"), 0.0))
            V(lambda e: e.memset(s_one1[:], 1.0))
            barrier()

            x0_ld = {}
            for c in range(NCH):
                cur, nxt = c % 2, (c + 1) % 2
                if c == 0:
                    for i in range(NT):
                        x0_ld[i] = c_sx[i % 2].inc()
                        if on("sync"):
                            eng.dma_start(out=bDf[:, i * 1024:(i + 1) * 1024],
                                          in_=x_h[i * 128:(i + 1) * 128, :]
                                          ).then_inc(m_sx[i % 2], 16)
                ph1_xln, ph1_evv = {}, {}
                for i in range(NT):
                    V(lambda e, ii=i: e.reduce_sum(out=s_stat[:, ii:ii + 1],
                                                   in_=bDf[:, ii * 1024:(ii + 1) * 1024],
                                                   axis=AX.X), (m_sx[i % 2], x0_ld[i]))
                a_sq = {}
                for i in range(NT):
                    a_sq[i] = A(lambda e, ii=i: e.activation(
                                    out=bCf[:, ii * 1024:(ii + 1) * 1024],
                                    in_=bDf[:, ii * 1024:(ii + 1) * 1024], func=AF.Square,
                                    accum_out=s_stat[:, 8 + ii:9 + ii]),
                                (m_sx[i % 2], x0_ld[i]))
                V(lambda e: e.tensor_scalar(out=s_stat[:, 4:8], in0=s_stat[:, 0:4],
                                            scalar1=1.0 / D, scalar2=None, op0=OP.mult),
                  (m_v, c_v.n))
                V(lambda e: e.tensor_tensor(out=s_stat[:, 0:4], in0=s_stat[:, 4:8],
                                            in1=s_stat[:, 4:8], op=OP.mult), (m_v, c_v.n))
                V(lambda e: e.tensor_scalar(out=s_stat[:, 12:16], in0=s_stat[:, 8:12],
                                            scalar1=1.0 / D, scalar2=None, op0=OP.mult),
                  (m_a, a_sq[3]))
                v_var = V(lambda e: e.tensor_tensor(out=s_stat[:, 8:12], in0=s_stat[:, 12:16],
                                                    in1=s_stat[:, 0:4], op=OP.subtract),
                          (m_v, c_v.n))
                a_sd = A(lambda e: e.activation(out=s_stat[:, 12:16], in_=s_stat[:, 8:12],
                                                func=AF.Sqrt, bias=s_eps[:]), (m_v, v_var))
                v_rc = V(lambda e: e.reciprocal(out=s_stat[:, 0:4], in_=s_stat[:, 12:16]),
                         (m_a, a_sd))
                for i in range(NT):
                    xo = (bA1 if i < 2 else bA2).ap()[:, i % 2, :]
                    ph1_xln[i] = V(lambda e, ii=i, o=xo: e.tensor_scalar(
                                       out=o, in0=bDf[:, ii * 1024:(ii + 1) * 1024],
                                       scalar1=s_stat[:, 4 + ii:5 + ii],
                                       scalar2=s_stat[:, ii:ii + 1],
                                       op0=OP.subtract, op1=OP.mult),
                                   (m_v, v_rc if i == 0 else 0))
                for i in range(NT):
                    for dg in range(2):
                        kg = i * 2 + dg
                        ptx = (pt0, pt1, pt2, pt3)[kg % 4]
                        for s4 in range(4):
                            dt = dg * 4 + s4
                            src_ap = (bA1 if i < 2 else bA2).ap()[:, i % 2, dt * 128:(dt + 1) * 128]
                            v_t = P(lambda e, s=src_ap, p=ptx, q=s4: e.matmul(
                                        out=p[:, q * 128:(q + 1) * 128], lhsT=s, rhs=s_id[:],
                                        start=(q == 0), stop=(q == 3), is_transpose=True), "tr",
                                    (m_v, ph1_xln[i] if s4 == 0 else 0),
                                    (m_a, ph1_evv.get(kg - 4, 0) if s4 == 0 else 0))
                        dst_b = bB.ap()[:, dg * 4:(dg + 1) * 4, i * 128:(i + 1) * 128]
                        v_ev = A(lambda e, d=dst_b, p=ptx: e.activation(
                                     out=d, in_=p.ap().rearrange("p (a b) -> p a b", b=128),
                                     func=AF.Copy), (m_p, v_t))
                        ph1_evv[kg] = v_ev
                a_bl_last = c_a.n

                v_xp = 0
                for kc in range(NDT):
                    v_xp = P(lambda e, k=kc: e.matmul(out=psA[0:E, :], lhsT=s_xpw.ap()[:, k, :],
                                                      rhs=bB.ap()[:, k, :], start=(k == 0),
                                                      stop=(k == NDT - 1)), "mf",
                             (m_a, a_bl_last if kc == 0 else 0))
                v_dr = A(lambda e: e.activation(out=s_drank[0:RNK, :], in_=psA[0:RNK, :],
                                                func=AF.Copy), (m_p, v_xp))
                v_bce = A(lambda e: e.activation(out=s_row.ap()[0:32, 0, :],
                                                 in_=psA[RNK:E, :], func=AF.Copy))
                c_g.inc()
                g_bc = c_g.n
                if on("gpsimd"):
                    eng.wait_ge(m_a, v_bce)
                    eng.dma_start(out=bc_d[c], in_=s_row.ap()[0:32, 0, :]).then_inc(m_g, 16)
                bc_tgts = []
                for n in range(NST):
                    bc_tgts.append(bE.ap()[:, n, :] if n < 8 else bJ.ap()[:, n - 8, :])
                for n in range(NST):
                    if n < 8:
                        bc_tgts.append(bF.ap()[:, n, :])
                    elif n < 12:
                        bc_tgts.append(bA1f[:, (n - 8) * TC:(n - 7) * TC])
                    else:
                        bc_tgts.append(bA2f[:, (n - 12) * TC:(n - 11) * TC])
                row_order = list(range(0, 8)) + list(range(16, 24)) + \
                            list(range(8, 16)) + list(range(24, 32))
                a_exp, a_ln = {}, {}
                bc_a, repl_p = {}, {}
                da_pre = {}

                def emit_da(d, oc2, first_wait=0):
                    vals = []
                    for j in range(8):
                        n = oc2 * 8 + j
                        oslot = (d * 2 + oc2) % 2
                        v = A(lambda e, dd=d, nn=n, os=oslot, jj=j: e.activation(
                                  out=s_dA.ap()[:, os, jj, :], in_=bC.ap()[:, dd, :],
                                  func=AF.Exp,
                                  scale=s_A[:, dd * NST + nn: dd * NST + nn + 1]),
                              (m_v, first_wait if j == 0 else 0))
                        vals.append(v)
                    return vals

                def emit_z(dt):
                    psz = psB if dt % 2 == 0 else psC
                    v_z = P(lambda e, d=dt, p=psz: e.matmul(
                                out=p[:], lhsT=s_dtw[:, d * 128:(d + 1) * 128],
                                rhs=s_drank[:], start=True, stop=True), "mm",
                            (m_a, v_dr if dt == 0 else 0), (m_a, a_exp.get(dt - 2, 0)))
                    a_exp[dt] = A(lambda e, d=dt, p=psz: e.activation(
                                      out=bD.ap()[:, d, :], in_=p[:],
                                      func=AF.Exp, bias=s_dtb[:, d:d + 1]), (m_p, v_z))
                    a_ln[dt] = A(lambda e, d=dt: e.activation(out=bC.ap()[:, d, :],
                                                              in_=bD.ap()[:, d, :],
                                                              func=AF.Ln, bias=s_one1[:]))

                grp_val = {}

                def emit_grp_dma(g):
                    base = row_order[8 * g]
                    grp_val[g] = c_r[g % 2].inc()
                    if on("sync"):
                        if g < 2:
                            eng.wait_ge(m_g, g_bc)
                        else:
                            eng.wait_ge(m_p, repl_p[8 * g - 9])
                        eng.dma_start(out=s_row.ap()[0:8, g % 2, :],
                                      in_=bc_d[c][base:base + 8, :]
                                      ).then_inc(m_r[g % 2], 16)

                def emit_repl(g, on_v):
                    for idx in range(8 * g, 8 * g + 8):
                        r = row_order[idx]
                        ptx = (pt0, pt1, pt2, pt3)[idx % 4]
                        pw = bc_a.get(idx - 4, (m_a, 0))
                        repl_p[idx] = P(lambda e, ii=idx, p=ptx: e.matmul(
                                            out=p[:], lhsT=s_sel8.ap()[:, ii % 8, :],
                                            rhs=s_row.ap()[:, (ii // 8) % 2, :],
                                            start=True, stop=True), "mm",
                                        (m_r[g % 2], grp_val[g]), pw,
                                        (m_a, ph1_evv[4 + idx] if idx < 4 else 0))
                        if on_v:
                            bc_a[idx] = (m_v, V(lambda e, rr=r, p=ptx: e.tensor_copy(
                                                    out=bc_tgts[rr], in_=p[:]),
                                                (m_p, repl_p[idx])))
                        else:
                            bc_a[idx] = (m_a, A(lambda e, rr=r, p=ptx: e.activation(
                                                    out=bc_tgts[rr], in_=p[:], func=AF.Copy),
                                                (m_p, repl_p[idx])))

                emit_z(0)
                emit_z(1)
                V(lambda e: e.tensor_tensor(out=bD.ap()[:, 0, :], in0=bC.ap()[:, 0, :],
                                            in1=bB.ap()[:, 0, :], op=OP.mult),
                  (m_a, a_ln[0]))
                emit_grp_dma(0)
                emit_grp_dma(1)
                emit_repl(0, True)
                da_pre[(0, 0)] = emit_da(0, 0)
                emit_z(2)
                emit_z(3)
                emit_repl(1, False)
                da_pre[(0, 1)] = emit_da(0, 1)
                emit_z(4)
                emit_z(5)
                emit_z(6)
                emit_z(7)
                need_c = {0: bc_a[15]}

                k_scan = 0
                scan_v = {}
                stt_vals = {}
                bHr = bH.ap().rearrange("p j t -> p t j")
                bA1q = bA1.ap().rearrange("p a (c f) -> p (a c) f", f=TC)
                bA2q = bA2.ap().rearrange("p a (c f) -> p (a c) f", f=TC)
                for dt in range(NDT):
                    if dt > 0:
                        V(lambda e, d=dt: e.tensor_tensor(out=bD.ap()[:, d, :],
                                                          in0=bC.ap()[:, d, :],
                                                          in1=bB.ap()[:, d, :], op=OP.mult),
                          (m_a, a_ln[dt]))
                    for oc2 in range(2):
                        if dt == 0 and oc2 == 1:
                            emit_grp_dma(2)
                            emit_grp_dma(3)
                            emit_repl(2, True)
                            emit_repl(3, False)
                            need_c[1] = bc_a[31]
                        goct = dt * 2 + oc2
                        boct = bE.ap() if oc2 == 0 else bJ.ap()
                        for jq in range(8):
                            V(lambda e, d=dt, bo=boct, jj=jq: e.tensor_tensor(
                                  out=bH.ap()[:, jj, :], in0=bD.ap()[:, d, :],
                                  in1=bo[:, jj, :], op=OP.mult))
                        if (dt, oc2) in da_pre:
                            da_vals = da_pre[(dt, oc2)]
                        else:
                            da_vals = emit_da(dt, oc2, scan_v.get((goct - 2, 7), 0))
                        for j in range(8):
                            n = oc2 * 8 + j
                            oslot = goct % 2
                            v_sc = V(lambda e, d=dt, nn=n, os=oslot, jj=j: e.tensor_tensor_scan(
                                         out=bH.ap()[:, jj, :], data0=s_dA.ap()[:, os, jj, :],
                                         data1=bH.ap()[:, jj, :],
                                         initial=s_carry.ap()[:, cur, d * NST + nn: d * NST + nn + 1],
                                         op0=OP.mult, op1=OP.add),
                                     (m_a, da_vals[7] if j == 0 else 0))
                            scan_v[(goct, j)] = v_sc
                            k_scan += 1
                        cb = dt * NST + oc2 * 8
                        V(lambda e, b=cb: e.tensor_copy(
                              out=s_carry.ap()[:, nxt, b:b + 4],
                              in_=bH.ap()[:, 0:4, TC - 1]))
                        V(lambda e: e.tensor_tensor(out=bH.ap()[:, 0:4, :],
                                                    in0=bH.ap()[:, 0:4, :],
                                                    in1=bF.ap()[:, 0:4, :] if oc2 == 0 else bA1q,
                                                    op=OP.mult),
                          need_c[oc2] if dt == 0 else (m_a, 0))
                        V(lambda e, b=cb: e.tensor_copy(
                              out=s_carry.ap()[:, nxt, b + 4:b + 8],
                              in_=bH.ap()[:, 4:8, TC - 1]))
                        V(lambda e: e.tensor_tensor(out=bH.ap()[:, 4:8, :],
                                                    in0=bH.ap()[:, 4:8, :],
                                                    in1=bF.ap()[:, 4:8, :] if oc2 == 0 else bA2q,
                                                    op=OP.mult))
                        V(lambda e: e.tensor_tensor(out=bH.ap()[:, 0:4, :], in0=bH.ap()[:, 0:4, :],
                                                    in1=bH.ap()[:, 4:8, :], op=OP.add))
                        V(lambda e: e.tensor_tensor(out=bH.ap()[:, 0:2, :], in0=bH.ap()[:, 0:2, :],
                                                    in1=bH.ap()[:, 2:4, :], op=OP.add))
                        if oc2 == 0:
                            V(lambda e: e.tensor_tensor(out=s_yq.ap()[:, 0, :],
                                                        in0=bH.ap()[:, 0, :],
                                                        in1=bH.ap()[:, 1, :], op=OP.add),
                              (m_v, c_v.n))
                        else:
                            V(lambda e: e.tensor_tensor(out=bH.ap()[:, 0, :],
                                                        in0=bH.ap()[:, 0, :],
                                                        in1=bH.ap()[:, 1, :], op=OP.add),
                              (m_v, c_v.n))
                    V(lambda e: e.tensor_tensor(out=bH.ap()[:, 0, :], in0=bH.ap()[:, 0, :],
                                                in1=s_yq.ap()[:, 0, :], op=OP.add), (m_v, c_v.n))
                    stt_v = V(lambda e, d=dt: e.scalar_tensor_tensor(out=bD.ap()[:, d, :],
                                                             in0=bB.ap()[:, d, :],
                                                             scalar=s_D[:, d:d + 1],
                                                             in1=bH.ap()[:, 0, :],
                                                             op0=OP.mult, op1=OP.add),
                              (m_v, c_v.n))
                    stt_vals[dt] = stt_v
                v_ph3b_end = c_v.n

                ph4_tt = {}
                for i in range(NT):
                    v_ld = c_sx[i % 2].inc()
                    if on("sync"):
                        eng.wait_ge(m_v, stt_vals[i * 2 + 1])
                        eng.dma_start(out=bBf[:, i * 1024:(i + 1) * 1024],
                                      in_=x_h[c * TC + i * 128: c * TC + (i + 1) * 128, :]
                                      ).then_inc(m_sx[i % 2], 16)
                    for dg in range(2):
                        kg = i * 2 + dg
                        ptx = (pt0, pt1, pt2, pt3)[kg % 4]
                        prev_tt = ph4_tt.get((i, dg - 2) if dg >= 2 else (i - 1, dg), 0)
                        for s4 in range(4):
                            dt = dg * 4 + s4
                            ptw = (bc_a[28], bc_a[29], bc_a[30], bc_a[31])[kg] \
                                if kg < 4 and s4 == 0 else (m_a, 0)
                            v_t = P(lambda e, d=dt, ii=i, p=ptx, q=s4: e.matmul(
                                        out=p[:, q * 128:(q + 1) * 128],
                                        lhsT=bD.ap()[:, d, ii * 128:(ii + 1) * 128], rhs=s_id[:],
                                        start=(q == 0), stop=(q == 3), is_transpose=True), "tr",
                                    (m_v, prev_tt if s4 == 0 else 0),
                                    (m_v, stt_vals[dt] if kg < 4 else 0),
                                    ptw)
                        v_tt = V(lambda e, d0=dg * 4, ii=i, p=ptx: e.tensor_tensor(
                                     out=bJf[:, ii * 1024 + d0 * 128: ii * 1024 + (d0 + 4) * 128],
                                     in0=p[:],
                                     in1=bBf[:, ii * 1024 + d0 * 128: ii * 1024 + (d0 + 4) * 128],
                                     op=OP.add),
                                 (m_p, v_t), (m_sx[i % 2], v_ld))
                        ph4_tt[(i, dg)] = v_tt
                v_da_last = c_a.n

                for i in range(NT):
                    V(lambda e, ii=i: e.reduce_sum(out=s_stat[:, ii:ii + 1],
                                                   in_=bJf[:, ii * 1024:(ii + 1) * 1024],
                                                   axis=AX.X))
                a_sq2 = {}
                for i in range(NT):
                    a_sq2[i] = A(lambda e, ii=i: e.activation(
                                     out=bA2.ap()[:, ii % 2, :],
                                     in_=bJf[:, ii * 1024:(ii + 1) * 1024], func=AF.Square,
                                     accum_out=s_stat[:, 8 + ii:9 + ii]),
                                 (m_v, ph4_tt[(i, 1)]))
                V(lambda e: e.tensor_scalar(out=s_stat[:, 4:8], in0=s_stat[:, 0:4],
                                            scalar1=1.0 / D, scalar2=None, op0=OP.mult),
                  (m_v, c_v.n))
                V(lambda e: e.tensor_tensor(out=s_stat[:, 0:4], in0=s_stat[:, 4:8],
                                            in1=s_stat[:, 4:8], op=OP.mult), (m_v, c_v.n))
                V(lambda e: e.tensor_scalar(out=s_stat[:, 12:16], in0=s_stat[:, 8:12],
                                            scalar1=1.0 / D, scalar2=None, op0=OP.mult),
                  (m_a, a_sq2[3]))
                v_var2 = V(lambda e: e.tensor_tensor(out=s_stat[:, 8:12], in0=s_stat[:, 12:16],
                                                     in1=s_stat[:, 0:4], op=OP.subtract),
                           (m_v, c_v.n))
                a_sd2 = A(lambda e: e.activation(out=s_stat[:, 12:16], in_=s_stat[:, 8:12],
                                                 func=AF.Sqrt, bias=s_eps[:]), (m_v, v_var2))
                v_rc2 = V(lambda e: e.reciprocal(out=s_stat[:, 0:4], in_=s_stat[:, 12:16]),
                          (m_a, a_sd2))
                for i in range(NT):
                    V(lambda e, ii=i: e.tensor_scalar(
                          out=bCf[:, ii * 1024:(ii + 1) * 1024],
                          in0=bJf[:, ii * 1024:(ii + 1) * 1024],
                          scalar1=s_stat[:, 4 + ii:5 + ii], scalar2=s_stat[:, ii:ii + 1],
                          op0=OP.subtract, op1=OP.mult),
                      (m_v, v_rc2 if i == 0 else 0), (m_a, v_da_last if i == 0 else 0))
                for i in range(NT):
                    V(lambda e, ii=i: e.tensor_reduce(out=s_stat[:, 8 + ii:9 + ii],
                                                      in_=bCf[:, ii * 1024:(ii + 1) * 1024],
                                                      axis=AX.X, op=OP.max,
                                                      apply_absolute_value=True))
                V(lambda e: e.tensor_scalar(out=s_xs[:, 0:4], in0=s_stat[:, 8:12],
                                            scalar1=1e-6, scalar2=1.0 / 127.0,
                                            op0=OP.max, op1=OP.mult), (m_v, c_v.n))
                v_inv = V(lambda e: e.reciprocal(out=s_inv[:, 0:4], in_=s_xs[:, 0:4]),
                          (m_v, c_v.n))
                for i in range(NT):
                    V(lambda e, ii=i: e.tensor_scalar(out=bEf[:, ii * 1024:(ii + 1) * 1024],
                                                      in0=bCf[:, ii * 1024:(ii + 1) * 1024],
                                                      scalar1=s_inv[:, ii:ii + 1], scalar2=128.5,
                                                      op0=OP.mult, op1=OP.add),
                      (m_v, v_inv if i == 0 else 0))
                xqf_h = {}
                for h in range(2):
                    hs = slice(h * 2048, (h + 1) * 2048)
                    V(lambda e, s=hs: e.tensor_scalar(out=bHf[:, s], in0=bEf[:, s],
                                                      scalar1=255.99, scalar2=0.01,
                                                      op0=OP.min, op1=OP.max))
                    src = bHf
                    dsts = [bEf, bFf]
                    for bi, b in enumerate([128.0, 64.0, 32.0, 16.0, 8.0, 4.0, 2.0, 1.0]):
                        dst = dsts[bi % 2]
                        V(lambda e, sc=src, bb=b, s=hs: e.tensor_scalar(
                              out=bBf[:, s], in0=sc[:, s], scalar1=bb, scalar2=bb,
                              op0=OP.is_ge, op1=OP.mult))
                        V(lambda e, sc=src, d=dst, s=hs: e.tensor_tensor(
                              out=d[:, s], in0=sc[:, s], in1=bBf[:, s], op=OP.subtract))
                        src = dst
                    xqf_h[h] = V(lambda e, sc=src, s=hs: e.scalar_tensor_tensor(
                                     out=bBf[:, s], in0=bHf[:, s], scalar=-128.0,
                                     in1=sc[:, s], op0=OP.add, op1=OP.subtract))

                ph6_ev = {}
                e1_v = {}
                e1all_h = {}
                ph6_ev2 = {}

                def emit_xtr(h):
                    for i in (2 * h, 2 * h + 1):
                        for dg in range(2):
                            kg = i * 2 + dg
                            ptx = (pt0, pt1, pt2, pt3)[kg % 4]
                            for s4 in range(4):
                                dt = dg * 4 + s4
                                v_t = P(lambda e, ii=i, d=dt, p=ptx, q=s4: e.matmul(
                                            out=p[:, q * 128:(q + 1) * 128],
                                            lhsT=bBf[:, ii * 1024 + d * 128:
                                                      ii * 1024 + (d + 1) * 128],
                                            rhs=s_id[:], start=(q == 0), stop=(q == 3),
                                            is_transpose=True), "tr",
                                        (m_v, xqf_h[h] if i == 2 * h and dg == 0 and s4 == 0
                                         else 0),
                                        (m_a, ph6_ev.get(kg - 4, 0) if s4 == 0 else 0))
                            v_ev = A(lambda e, ii=i, d0=dg * 4, p=ptx: e.activation(
                                         out=bL.ap()[:, d0:d0 + 4, ii * 128:(ii + 1) * 128],
                                         in_=p.ap().rearrange("p (a b) -> p a b", b=128),
                                         func=AF.Copy), (m_p, v_t))
                            ph6_ev[kg] = v_ev

                def emit_e1mm(h):
                    ts = slice(h * 256, (h + 1) * 256)
                    for oc in range(NOC):
                        psx = psA if oc % 2 == 0 else psB
                        v_mm = 0
                        for kc in range(NDT):
                            v_mm = P(lambda e, o=oc, k=kc, p=psx, s=ts, hh=h: e.matmul(
                                         out=p[:, s], lhsT=s_wq.ap()[:, k, o * 128:(o + 1) * 128],
                                         rhs=bL.ap()[:, k, hh * 256:(hh + 1) * 256],
                                         start=(k == 0), stop=(k == NDT - 1)),
                                     "mm", (m_a, ph6_ev[4 * h + 3] if kc == 0 else 0),
                                     (m_a, v_bce if h == 0 and oc == 0 and kc == 0 else 0),
                                     (m_a, e1_v.get((oc - 2, h), 0) if kc == 0 else 0),
                                     (m_a, e1_v.get((6 + oc % 2, 0), 0)
                                      if h == 1 and kc == 0 else 0))
                        v_e1 = A(lambda e, o=oc, p=psx, s=ts, hh=h: e.activation(
                                     out=bE.ap()[:, o, hh * 256:(hh + 1) * 256], in_=p[:, s],
                                     func=AF.Copy, scale=s_ws[:, o:o + 1]),
                                 (m_p, v_mm),
                                 (m_v, xqf_h[1] if h == 0 and oc >= 4 else 0))
                        e1_v[(oc, h)] = v_e1
                    e1all_h[h] = c_a.n

                def emit_e1t(i):
                    for og in range(2):
                        kg = i * 2 + og
                        ptx = (pt0, pt1, pt2, pt3)[kg % 4]
                        for s4 in range(4):
                            oc = og * 4 + s4
                            v_t = P(lambda e, o=oc, ii=i, p=ptx, q=s4: e.matmul(
                                        out=p[:, q * 128:(q + 1) * 128],
                                        lhsT=bE.ap()[:, o, ii * 128:(ii + 1) * 128],
                                        rhs=s_id[:], start=(q == 0), stop=(q == 3),
                                        is_transpose=True), "tr",
                                    (m_a, e1all_h[i // 2] if og == 0 and s4 == 0 else 0),
                                    (m_a, ph6_ev.get(kg + 4, 0) if i < 2 and s4 == 0 else 0),
                                    (m_a, ph6_ev2.get(kg - 4, 0) if s4 == 0 else 0))
                        v_ev = A(lambda e, o0=og * 4, ii=i, p=ptx: e.activation(
                                     out=bDf[:, ii * 1024 + o0 * 128: ii * 1024 + (o0 + 4) * 128],
                                     in_=p[:], func=AF.Copy), (m_p, v_t))
                        ph6_ev2[kg] = v_ev

                emit_xtr(0)
                emit_e1mm(0)
                emit_xtr(1)
                emit_e1t(0)
                emit_e1t(1)
                emit_e1mm(1)
                emit_e1t(2)
                emit_e1t(3)
                for i in range(NT):
                    g = c * NT + i
                    v_stt = V(lambda e, ii=i: e.scalar_tensor_tensor(
                                  out=bA2.ap()[:, ii % 2, :], in0=bDf[:, ii * 1024:(ii + 1) * 1024],
                                  scalar=s_xs[:, ii:ii + 1], in1=s_biasrep[:],
                                  op0=OP.mult, op1=OP.add),
                              (m_o[g % 2], c_o[g % 2].n if g >= 2 else 0),
                              (m_a, ph6_ev2[i * 2 + 1]))
                    if i < 2:
                        Q(lambda e, ii=i: e.tensor_tensor(
                              out=bA1.ap()[:, ii % 2, :], in0=bA2.ap()[:, ii % 2, :],
                              in1=bJf[:, ii * 1024:(ii + 1) * 1024], op=OP.add),
                          (m_v, v_stt))
                        v_st = c_o[g % 2].inc()
                        if on("gpsimd"):
                            eng.dma_start(out=out_h[c * TC + i * 128: c * TC + (i + 1) * 128, :],
                                          in_=bA1.ap()[:, i % 2, :]).then_inc(m_o[g % 2], 16)
                    else:
                        v_t1 = V(lambda e, ii=i: e.tensor_tensor(
                                     out=bA1.ap()[:, ii % 2, :], in0=bA2.ap()[:, ii % 2, :],
                                     in1=bJf[:, ii * 1024:(ii + 1) * 1024], op=OP.add))
                        v_st = c_o[g % 2].inc()
                        if on("gpsimd"):
                            eng.wait_ge(m_v, v_t1)
                            eng.dma_start(out=out_h[c * TC + i * 128: c * TC + (i + 1) * 128, :],
                                          in_=bA1.ap()[:, i % 2, :]).then_inc(m_o[g % 2], 16)
                    if c + 1 < NCH:
                        x0_ld[i] = c_sx[i % 2].inc()
                        if on("sync"):
                            eng.wait_ge(m_v, v_stt)
                            eng.dma_start(out=bDf[:, i * 1024:(i + 1) * 1024],
                                          in_=x_h[(c + 1) * TC + i * 128:
                                                  (c + 1) * TC + (i + 1) * 128, :]
                                          ).then_inc(m_sx[i % 2], 16)
                if on("gpsimd"):
                    wait_all_out(eng)
                barrier()

            if on("gpsimd"):
                wait_all_out(eng)
            barrier()

        with nc.Block() as block:
            @block.sync
            def _(eng):
                emit("sync", eng)

            @block.vector
            def _(eng):
                emit("vector", eng)

            @block.scalar
            def _(eng):
                emit("scalar", eng)

            @block.tensor
            def _(eng):
                emit("tensor", eng)

            @block.gpsimd
            def _(eng):
                emit("gpsimd", eng)

    return nc


_NC_CACHE = {}


def _get_nc():
    if "nc" not in _NC_CACHE:
        _NC_CACHE["nc"] = build_nc()
    return _NC_CACHE["nc"]


def kernel(L0_attn_x, L0_attn_A_log, L0_attn_D, L0_attn_x_proj_w,
           L0_attn_dt_proj_w, L0_attn_dt_proj_b, L0_ffn_weight_q,
           L0_ffn_weight_scale, L0_ffn_bias):
    x = np.asarray(L0_attn_x, dtype=np.float32)
    B = x.shape[0]
    xpw_t = np.ascontiguousarray(np.asarray(L0_attn_x_proj_w, np.float32).T)
    dtw_t = np.ascontiguousarray(np.asarray(L0_attn_dt_proj_w, np.float32).T)
    dtb_dm = np.ascontiguousarray(np.asarray(L0_attn_dt_proj_b, np.float32).reshape(NDT, 128).T)
    alog_dm = np.ascontiguousarray(
        np.asarray(L0_attn_A_log, np.float32).reshape(NDT, 128, NST).transpose(1, 0, 2).reshape(128, 128))
    dvec_dm = np.ascontiguousarray(np.asarray(L0_attn_D, np.float32).reshape(NDT, 128).T)
    wqt = np.ascontiguousarray(
        np.asarray(L0_ffn_weight_q).T.reshape(NDT, 128, D).transpose(1, 0, 2))
    wsc_om = np.ascontiguousarray(np.asarray(L0_ffn_weight_scale, np.float32).reshape(NOC, 128).T)
    biasr = np.ascontiguousarray(
        np.broadcast_to(np.asarray(L0_ffn_bias, np.float32).reshape(1, D), (128, D)))
    ident = np.eye(128, dtype=np.float32)
    onesm = np.zeros((128, 128), dtype=np.float32)
    onesm[0, :] = 1.0
    sel8 = np.zeros((128, 8, 128), dtype=np.float32)
    for r in range(8):
        sel8[r, r, :] = 1.0

    shared = dict(xpw=xpw_t, dtw=dtw_t, dtb=dtb_dm, alog=alog_dm, dvec=dvec_dm,
                  wqt=wqt, wsc=wsc_om, biasr=biasr, ident=ident, onesm=onesm,
                  sel8=sel8)
    in_maps = [dict(x=np.ascontiguousarray(x[b]), **shared) for b in range(B)]

    nc = _get_nc()
    res = run_bass_kernel_spmd(nc, in_maps, core_ids=list(range(B)))
    out = np.stack([np.asarray(r["out"]) for r in res.results], axis=0)
    return out


# version 21
# speedup vs baseline: 1.0068x; 1.0068x over previous
import contextlib

import numpy as np
import ml_dtypes
import concourse.bass as bass
import concourse.mybir as mybir
from concourse.bass_utils import run_bass_kernel_spmd

F32 = mybir.dt.float32
BF16 = mybir.dt.bfloat16
I8 = mybir.dt.int8
AF = mybir.ActivationFunctionType
OP = mybir.AluOpType
AX = mybir.AxisListType

T = 4096
D = 1024
NST = 16
RNK = 64
E = 96
TC = 512
NCH = T // TC
NT = TC // 128
NDT = D // 128
NOC = 8


class Ctr:
    def __init__(self, step=1):
        self.n, self.step = 0, step

    def inc(self):
        self.n += self.step
        return self.n


def build_nc():
    nc = bass.Bass("TRN2", target_bir_lowering=False, detect_race_conditions=False)

    x_h = nc.dram_tensor("x", [T, D], F32, kind="ExternalInput").ap()
    xpw_h = nc.dram_tensor("xpw", [D, E], F32, kind="ExternalInput").ap()
    dtw_h = nc.dram_tensor("dtw", [RNK, D], F32, kind="ExternalInput").ap()
    dtb_h = nc.dram_tensor("dtb", [128, NDT], F32, kind="ExternalInput").ap()
    alog_h = nc.dram_tensor("alog", [128, 128], F32, kind="ExternalInput").ap()
    dvec_h = nc.dram_tensor("dvec", [128, NDT], F32, kind="ExternalInput").ap()
    wqt_h = nc.dram_tensor("wqt", [128, NDT, D], I8, kind="ExternalInput").ap()
    wsc_h = nc.dram_tensor("wsc", [128, NOC], F32, kind="ExternalInput").ap()
    biasr_h = nc.dram_tensor("biasr", [128, D], F32, kind="ExternalInput").ap()
    ident_h = nc.dram_tensor("ident", [128, 128], F32, kind="ExternalInput").ap()
    ones_h = nc.dram_tensor("onesm", [128, 128], F32, kind="ExternalInput").ap()
    sel8_h = nc.dram_tensor("sel8", [128, 8, 128], F32, kind="ExternalInput").ap()
    out_h = nc.dram_tensor("out", [T, D], F32, kind="ExternalOutput").ap()
    bc_d = nc.dram_tensor("bc_dram", [NCH, 2 * NST, TC], F32, kind="Internal").ap()

    st = contextlib.ExitStack()
    sb = lambda name, shape, dt=F32: st.enter_context(nc.sbuf_tensor(name, shape, dt))
    psmk = lambda name, shape, dt=F32: st.enter_context(nc.psum_tensor(name, shape, dt))
    mksem = lambda name: st.enter_context(nc.semaphore(name))

    with st:
        s_xpw = sb("s_xpw", [128, NDT, E], BF16)
        s_dtw = sb("s_dtw", [128, D], BF16)
        s_wq = sb("s_wq", [128, NDT, D], BF16)
        s_A = sb("s_A", [128, 128])
        s_Ae = sb("s_Ae", [128, 128])
        s_dtb = sb("s_dtb", [128, NDT])
        s_D = sb("s_D", [128, NDT])
        s_ws = sb("s_ws", [128, NOC])
        s_biasrep = sb("s_biasrep", [128, D])
        s_id = sb("s_id", [128, 128])
        s_onesm = sb("s_onesm", [128, 128])
        s_carry = sb("s_carry", [128, 2, 128])
        s_stat = sb("s_stat", [128, 16])
        s_xs = sb("s_xs", [128, NT])
        s_inv = sb("s_inv", [128, NT])
        s_row = sb("s_row", [128, 2, TC])
        s_sel8 = sb("s_sel8", [128, 8, 128])
        s_eps = sb("s_eps", [128, 1])
        s_one1 = sb("s_one1", [128, 1])
        s_dA = sb("s_dA", [128, 2, 8, TC])
        s_yq = sb("s_yq", [128, 1, TC])
        s_drank = sb("s_drank", [128, TC], BF16)
        bA1 = sb("bA1", [128, 2, 1024])
        bA2 = sb("bA2", [128, 2, 1024])
        bB = sb("bB", [128, NDT, TC])
        bC = sb("bC", [128, NDT, TC])
        bD = sb("bD", [128, NDT, TC])
        bE = sb("bE", [128, 8, TC])
        bF = sb("bF", [128, 8, TC])
        bH = sb("bH", [128, 8, TC])
        bJ = sb("bJ", [128, 8, TC])
        bL = sb("bL", [128, NDT, TC], BF16)
        psA = psmk("psA", [128, TC])
        psB = psmk("psB", [128, TC])
        psC = psmk("psC", [128, TC])
        pt0 = psmk("pt0", [128, TC])
        pt1 = psmk("pt1", [128, TC])
        pt2 = psmk("pt2", [128, TC])
        pt3 = psmk("pt3", [128, TC])

        bA1f = bA1.ap().rearrange("p a b -> p (a b)")
        bA2f = bA2.ap().rearrange("p a b -> p (a b)")
        bEf = bE.ap().rearrange("p a b -> p (a b)")
        bFf = bF.ap().rearrange("p a b -> p (a b)")
        bHf = bH.ap().rearrange("p a b -> p (a b)")
        bBf = bB.ap().rearrange("p a b -> p (a b)")
        bCf = bC.ap().rearrange("p a b -> p (a b)")
        bDf = bD.ap().rearrange("p a b -> p (a b)")
        bJf = bJ.ap().rearrange("p a b -> p (a b)")

        bar = mksem("bar")
        m_v = mksem("m_v")
        m_a = mksem("m_a")
        m_p = mksem("m_p")
        m_g = mksem("m_g")
        m_o0 = mksem("m_o0")
        m_o1 = mksem("m_o1")
        m_s0 = mksem("m_s0")
        m_s1 = mksem("m_s1")
        m_r0 = mksem("m_r0")
        m_r1 = mksem("m_r1")
        m_q = mksem("m_q")

        def emit(en, eng):
            on = lambda e: en == e
            nbar = [0]
            c_v, c_a, c_p, c_q = Ctr(), Ctr(), Ctr(), Ctr()
            c_g = Ctr(16)
            c_o = [Ctr(16), Ctr(16)]
            m_o = [m_o0, m_o1]
            c_sx = [Ctr(16), Ctr(16)]
            m_sx = [m_s0, m_s1]
            c_r = [Ctr(16), Ctr(16)]
            m_r = [m_r0, m_r1]
            pe_mode = [None]

            def V(fn, *waits):
                val = c_v.inc()
                if on("vector"):
                    for s, v in waits:
                        if v > 0:
                            eng.wait_ge(s, v)
                    fn(eng).then_inc(m_v, 1)
                return val

            def A(fn, *waits):
                val = c_a.inc()
                if on("scalar"):
                    for s, v in waits:
                        if v > 0:
                            eng.wait_ge(s, v)
                    fn(eng).then_inc(m_a, 1)
                return val

            def Q(fn, *waits):
                prev = c_q.n
                val = c_q.inc()
                if on("gpsimd"):
                    for s, v in waits:
                        if v > 0:
                            eng.wait_ge(s, v)
                    if prev > 0:
                        eng.wait_ge(m_q, prev)
                    fn(eng).then_inc(m_q, 1)
                return val

            def P(fn, mode, *waits):
                sw = pe_mode[0] is not None and pe_mode[0] != mode
                pe_mode[0] = mode
                val = c_p.inc()
                if on("tensor"):
                    if sw:
                        eng.drain()
                    for s, v in waits:
                        if v > 0:
                            eng.wait_ge(s, v)
                    fn(eng).then_inc(m_p, 1)
                return val

            def barrier():
                nbar[0] += 1
                if en == "vector":
                    eng.wait_ge(m_v, c_v.n)
                elif en == "scalar":
                    eng.wait_ge(m_a, c_a.n)
                elif en == "tensor":
                    eng.wait_ge(m_p, c_p.n)
                elif en == "sync":
                    eng.wait_ge(m_s0, c_sx[0].n)
                    eng.wait_ge(m_s1, c_sx[1].n)
                    eng.wait_ge(m_r0, c_r[0].n)
                    eng.wait_ge(m_r1, c_r[1].n)
                else:
                    eng.wait_ge(m_g, c_g.n)
                    eng.wait_ge(m_q, c_q.n)
                    eng.wait_ge(m_o0, c_o[0].n)
                    eng.wait_ge(m_o1, c_o[1].n)
                eng.sem_inc(bar, 1)
                eng.wait_ge(bar, 5 * nbar[0])

            def wait_all_out(e):
                e.wait_ge(m_o0, c_o[0].n)
                e.wait_ge(m_o1, c_o[1].n)

            for tgt, src in ((s_xpw[:], xpw_h.rearrange("(c p) e -> p c e", p=128)),
                             (s_dtw[0:RNK, :], dtw_h), (s_wq[:], wqt_h)):
                c_g.inc()
                if on("gpsimd"):
                    eng.dma_start(out=tgt, in_=src).then_inc(m_g, 16)
            for tgt, src in ((s_Ae[:], alog_h), (s_dtb[:], dtb_h), (s_D[:], dvec_h),
                             (s_ws[:], wsc_h), (s_biasrep[:], biasr_h),
                             (s_id[:], ident_h), (s_onesm[:], ones_h),
                             (s_sel8[:], sel8_h)):
                c_sx[0].inc()
                if on("sync"):
                    eng.dma_start(out=tgt, in_=src).then_inc(m_s0, 16)
            if on("gpsimd"):
                eng.wait_ge(m_g, c_g.n)
            if on("sync"):
                eng.wait_ge(m_s0, c_sx[0].n)
            barrier()
            v_aexp = A(lambda e: e.activation(out=s_A[:], in_=s_Ae[:], func=AF.Exp))
            V(lambda e: e.tensor_scalar(out=s_A[:], in0=s_A[:], scalar1=-1.0,
                                        scalar2=None, op0=OP.mult), (m_a, v_aexp))
            V(lambda e: e.memset(s_carry.ap()[:, 0, :], 0.0))
            V(lambda e: e.memset(s_eps[:], 1e-5))
            V(lambda e: e.memset(s_dtw[RNK:128, :], 0.0))
            V(lambda e: e.memset(s_drank[RNK:128, :], 0.0))
            V(lambda e: e.memset(s_row.ap().rearrange("p a b -> p (a b)"), 0.0))
            V(lambda e: e.memset(s_one1[:], 1.0))
            barrier()

            x0_ld = {}
            for c in range(NCH):
                cur, nxt = c % 2, (c + 1) % 2
                if c == 0:
                    for i in range(NT):
                        x0_ld[i] = c_sx[i % 2].inc()
                        if on("sync"):
                            eng.dma_start(out=bDf[:, i * 1024:(i + 1) * 1024],
                                          in_=x_h[i * 128:(i + 1) * 128, :]
                                          ).then_inc(m_sx[i % 2], 16)
                ph1_xln, ph1_evv = {}, {}
                for i in range(NT):
                    V(lambda e, ii=i: e.reduce_sum(out=s_stat[:, ii:ii + 1],
                                                   in_=bDf[:, ii * 1024:(ii + 1) * 1024],
                                                   axis=AX.X), (m_sx[i % 2], x0_ld[i]))
                a_sq = {}
                for i in range(NT):
                    a_sq[i] = A(lambda e, ii=i: e.activation(
                                    out=bCf[:, ii * 1024:(ii + 1) * 1024],
                                    in_=bDf[:, ii * 1024:(ii + 1) * 1024], func=AF.Square,
                                    accum_out=s_stat[:, 8 + ii:9 + ii]),
                                (m_sx[i % 2], x0_ld[i]))
                V(lambda e: e.tensor_scalar(out=s_stat[:, 4:8], in0=s_stat[:, 0:4],
                                            scalar1=1.0 / D, scalar2=None, op0=OP.mult),
                  (m_v, c_v.n))
                V(lambda e: e.tensor_tensor(out=s_stat[:, 0:4], in0=s_stat[:, 4:8],
                                            in1=s_stat[:, 4:8], op=OP.mult), (m_v, c_v.n))
                V(lambda e: e.tensor_scalar(out=s_stat[:, 12:16], in0=s_stat[:, 8:12],
                                            scalar1=1.0 / D, scalar2=None, op0=OP.mult),
                  (m_a, a_sq[3]))
                v_var = V(lambda e: e.tensor_tensor(out=s_stat[:, 8:12], in0=s_stat[:, 12:16],
                                                    in1=s_stat[:, 0:4], op=OP.subtract),
                          (m_v, c_v.n))
                a_sd = A(lambda e: e.activation(out=s_stat[:, 12:16], in_=s_stat[:, 8:12],
                                                func=AF.Sqrt, bias=s_eps[:]), (m_v, v_var))
                v_rc = V(lambda e: e.reciprocal(out=s_stat[:, 0:4], in_=s_stat[:, 12:16]),
                         (m_a, a_sd))
                for i in range(NT):
                    xo = (bA1 if i < 2 else bA2).ap()[:, i % 2, :]
                    ph1_xln[i] = V(lambda e, ii=i, o=xo: e.tensor_scalar(
                                       out=o, in0=bDf[:, ii * 1024:(ii + 1) * 1024],
                                       scalar1=s_stat[:, 4 + ii:5 + ii],
                                       scalar2=s_stat[:, ii:ii + 1],
                                       op0=OP.subtract, op1=OP.mult),
                                   (m_v, v_rc if i == 0 else 0))
                for i in range(NT):
                    for dg in range(2):
                        kg = i * 2 + dg
                        ptx = (pt0, pt1, pt2, pt3)[kg % 4]
                        for s4 in range(4):
                            dt = dg * 4 + s4
                            src_ap = (bA1 if i < 2 else bA2).ap()[:, i % 2, dt * 128:(dt + 1) * 128]
                            v_t = P(lambda e, s=src_ap, p=ptx, q=s4: e.matmul(
                                        out=p[:, q * 128:(q + 1) * 128], lhsT=s, rhs=s_id[:],
                                        start=(q == 0), stop=(q == 3), is_transpose=True), "tr",
                                    (m_v, ph1_xln[i] if s4 == 0 else 0),
                                    (m_a, ph1_evv.get(kg - 4, 0) if s4 == 0 else 0))
                        dst_l = bL.ap()[:, dg * 4:(dg + 1) * 4, i * 128:(i + 1) * 128]
                        a_bl_last = A(lambda e, d=dst_l, p=ptx: e.activation(
                                          out=d, in_=p.ap().rearrange("p (a b) -> p a b", b=128),
                                          func=AF.Copy), (m_p, v_t))
                        dst_b = bB.ap()[:, dg * 4:(dg + 1) * 4, i * 128:(i + 1) * 128]
                        v_ev = A(lambda e, d=dst_b, p=ptx: e.activation(
                                     out=d, in_=p.ap().rearrange("p (a b) -> p a b", b=128),
                                     func=AF.Copy))
                        ph1_evv[kg] = v_ev

                v_xp = 0
                for kc in range(NDT):
                    v_xp = P(lambda e, k=kc: e.matmul(out=psA[0:E, :], lhsT=s_xpw.ap()[:, k, :],
                                                      rhs=bL.ap()[:, k, :], start=(k == 0),
                                                      stop=(k == NDT - 1)), "mm",
                             (m_a, a_bl_last if kc == 0 else 0))
                v_dr = A(lambda e: e.activation(out=s_drank[0:RNK, :], in_=psA[0:RNK, :],
                                                func=AF.Copy), (m_p, v_xp))
                v_bce = A(lambda e: e.activation(out=s_row.ap()[0:32, 0, :],
                                                 in_=psA[RNK:E, :], func=AF.Copy))
                c_g.inc()
                g_bc = c_g.n
                if on("gpsimd"):
                    eng.wait_ge(m_a, v_bce)
                    eng.dma_start(out=bc_d[c], in_=s_row.ap()[0:32, 0, :]).then_inc(m_g, 16)
                bc_tgts = []
                for n in range(NST):
                    bc_tgts.append(bE.ap()[:, n, :] if n < 8 else bJ.ap()[:, n - 8, :])
                for n in range(NST):
                    if n < 8:
                        bc_tgts.append(bF.ap()[:, n, :])
                    elif n < 12:
                        bc_tgts.append(bA1f[:, (n - 8) * TC:(n - 7) * TC])
                    else:
                        bc_tgts.append(bA2f[:, (n - 12) * TC:(n - 11) * TC])
                row_order = list(range(0, 8)) + list(range(16, 24)) + \
                            list(range(8, 16)) + list(range(24, 32))
                a_exp, a_ln = {}, {}
                bc_a, repl_p = {}, {}
                da_pre = {}

                def emit_da(d, oc2, first_wait=0):
                    vals = []
                    for j in range(8):
                        n = oc2 * 8 + j
                        oslot = (d * 2 + oc2) % 2
                        v = A(lambda e, dd=d, nn=n, os=oslot, jj=j: e.activation(
                                  out=s_dA.ap()[:, os, jj, :], in_=bC.ap()[:, dd, :],
                                  func=AF.Exp,
                                  scale=s_A[:, dd * NST + nn: dd * NST + nn + 1]),
                              (m_v, first_wait if j == 0 else 0))
                        vals.append(v)
                    return vals

                def emit_z(dt):
                    psz = psB if dt % 2 == 0 else psC
                    v_z = P(lambda e, d=dt, p=psz: e.matmul(
                                out=p[:], lhsT=s_dtw[:, d * 128:(d + 1) * 128],
                                rhs=s_drank[:], start=True, stop=True), "mm",
                            (m_a, v_dr if dt == 0 else 0), (m_a, a_exp.get(dt - 2, 0)))
                    a_exp[dt] = A(lambda e, d=dt, p=psz: e.activation(
                                      out=bD.ap()[:, d, :], in_=p[:],
                                      func=AF.Exp, bias=s_dtb[:, d:d + 1]), (m_p, v_z))
                    a_ln[dt] = A(lambda e, d=dt: e.activation(out=bC.ap()[:, d, :],
                                                              in_=bD.ap()[:, d, :],
                                                              func=AF.Ln, bias=s_one1[:]))

                grp_val = {}

                def emit_grp_dma(g):
                    base = row_order[8 * g]
                    grp_val[g] = c_r[g % 2].inc()
                    if on("sync"):
                        if g < 2:
                            eng.wait_ge(m_g, g_bc)
                        else:
                            eng.wait_ge(m_p, repl_p[8 * g - 9])
                        eng.dma_start(out=s_row.ap()[0:8, g % 2, :],
                                      in_=bc_d[c][base:base + 8, :]
                                      ).then_inc(m_r[g % 2], 16)

                def emit_repl(g, on_v):
                    for idx in range(8 * g, 8 * g + 8):
                        r = row_order[idx]
                        ptx = (pt0, pt1, pt2, pt3)[idx % 4]
                        pw = bc_a.get(idx - 4, (m_a, 0))
                        repl_p[idx] = P(lambda e, ii=idx, p=ptx: e.matmul(
                                            out=p[:], lhsT=s_sel8.ap()[:, ii % 8, :],
                                            rhs=s_row.ap()[:, (ii // 8) % 2, :],
                                            start=True, stop=True), "mm",
                                        (m_r[g % 2], grp_val[g]), pw,
                                        (m_a, ph1_evv[4 + idx] if idx < 4 else 0))
                        if on_v:
                            bc_a[idx] = (m_v, V(lambda e, rr=r, p=ptx: e.tensor_copy(
                                                    out=bc_tgts[rr], in_=p[:]),
                                                (m_p, repl_p[idx])))
                        else:
                            bc_a[idx] = (m_a, A(lambda e, rr=r, p=ptx: e.activation(
                                                    out=bc_tgts[rr], in_=p[:], func=AF.Copy),
                                                (m_p, repl_p[idx])))

                emit_z(0)
                emit_z(1)
                V(lambda e: e.tensor_tensor(out=bD.ap()[:, 0, :], in0=bC.ap()[:, 0, :],
                                            in1=bB.ap()[:, 0, :], op=OP.mult),
                  (m_a, a_ln[0]))
                emit_grp_dma(0)
                emit_grp_dma(1)
                emit_repl(0, True)
                da_pre[(0, 0)] = emit_da(0, 0)
                emit_z(2)
                emit_z(3)
                emit_repl(1, False)
                da_pre[(0, 1)] = emit_da(0, 1)
                emit_z(4)
                emit_z(5)
                emit_z(6)
                emit_z(7)
                need_c = {0: bc_a[15]}

                k_scan = 0
                scan_v = {}
                stt_vals = {}
                bHr = bH.ap().rearrange("p j t -> p t j")
                bA1q = bA1.ap().rearrange("p a (c f) -> p (a c) f", f=TC)
                bA2q = bA2.ap().rearrange("p a (c f) -> p (a c) f", f=TC)
                for dt in range(NDT):
                    if dt > 0:
                        V(lambda e, d=dt: e.tensor_tensor(out=bD.ap()[:, d, :],
                                                          in0=bC.ap()[:, d, :],
                                                          in1=bB.ap()[:, d, :], op=OP.mult),
                          (m_a, a_ln[dt]))
                    for oc2 in range(2):
                        if dt == 0 and oc2 == 1:
                            emit_grp_dma(2)
                            emit_grp_dma(3)
                            emit_repl(2, False)
                            emit_repl(3, False)
                            need_c[1] = bc_a[31]
                            need_b1 = bc_a[23]
                        goct = dt * 2 + oc2
                        boct = bE.ap() if oc2 == 0 else bJ.ap()
                        for jq in range(8):
                            V(lambda e, d=dt, bo=boct, jj=jq: e.tensor_tensor(
                                  out=bH.ap()[:, jj, :], in0=bD.ap()[:, d, :],
                                  in1=bo[:, jj, :], op=OP.mult),
                              need_b1 if dt == 0 and oc2 == 1 and jq == 0 else (m_a, 0))
                        if (dt, oc2) in da_pre:
                            da_vals = da_pre[(dt, oc2)]
                        else:
                            da_vals = emit_da(dt, oc2, scan_v.get((goct - 2, 7), 0))
                        for j in range(8):
                            n = oc2 * 8 + j
                            oslot = goct % 2
                            v_sc = V(lambda e, d=dt, nn=n, os=oslot, jj=j: e.tensor_tensor_scan(
                                         out=bH.ap()[:, jj, :], data0=s_dA.ap()[:, os, jj, :],
                                         data1=bH.ap()[:, jj, :],
                                         initial=s_carry.ap()[:, cur, d * NST + nn: d * NST + nn + 1],
                                         op0=OP.mult, op1=OP.add),
                                     (m_a, da_vals[7] if j == 0 else 0))
                            scan_v[(goct, j)] = v_sc
                            k_scan += 1
                        cb = dt * NST + oc2 * 8
                        V(lambda e, b=cb: e.tensor_copy(
                              out=s_carry.ap()[:, nxt, b:b + 4],
                              in_=bH.ap()[:, 0:4, TC - 1]))
                        V(lambda e: e.tensor_tensor(out=bH.ap()[:, 0:4, :],
                                                    in0=bH.ap()[:, 0:4, :],
                                                    in1=bF.ap()[:, 0:4, :] if oc2 == 0 else bA1q,
                                                    op=OP.mult),
                          need_c[oc2] if dt == 0 else (m_a, 0))
                        V(lambda e, b=cb: e.tensor_copy(
                              out=s_carry.ap()[:, nxt, b + 4:b + 8],
                              in_=bH.ap()[:, 4:8, TC - 1]))
                        V(lambda e: e.tensor_tensor(out=bH.ap()[:, 4:8, :],
                                                    in0=bH.ap()[:, 4:8, :],
                                                    in1=bF.ap()[:, 4:8, :] if oc2 == 0 else bA2q,
                                                    op=OP.mult))
                        V(lambda e: e.tensor_tensor(out=bH.ap()[:, 0:4, :], in0=bH.ap()[:, 0:4, :],
                                                    in1=bH.ap()[:, 4:8, :], op=OP.add))
                        V(lambda e: e.tensor_tensor(out=bH.ap()[:, 0:2, :], in0=bH.ap()[:, 0:2, :],
                                                    in1=bH.ap()[:, 2:4, :], op=OP.add))
                        if oc2 == 0:
                            V(lambda e: e.tensor_tensor(out=s_yq.ap()[:, 0, :],
                                                        in0=bH.ap()[:, 0, :],
                                                        in1=bH.ap()[:, 1, :], op=OP.add),
                              (m_v, c_v.n))
                        else:
                            V(lambda e: e.tensor_tensor(out=bH.ap()[:, 0, :],
                                                        in0=bH.ap()[:, 0, :],
                                                        in1=bH.ap()[:, 1, :], op=OP.add),
                              (m_v, c_v.n))
                    V(lambda e: e.tensor_tensor(out=bH.ap()[:, 0, :], in0=bH.ap()[:, 0, :],
                                                in1=s_yq.ap()[:, 0, :], op=OP.add), (m_v, c_v.n))
                    stt_v = V(lambda e, d=dt: e.scalar_tensor_tensor(out=bD.ap()[:, d, :],
                                                             in0=bB.ap()[:, d, :],
                                                             scalar=s_D[:, d:d + 1],
                                                             in1=bH.ap()[:, 0, :],
                                                             op0=OP.mult, op1=OP.add),
                              (m_v, c_v.n))
                    stt_vals[dt] = stt_v
                v_ph3b_end = c_v.n

                ph4_tt = {}
                for i in range(NT):
                    v_ld = c_sx[i % 2].inc()
                    if on("sync"):
                        eng.wait_ge(m_v, stt_vals[i * 2 + 1])
                        eng.dma_start(out=bBf[:, i * 1024:(i + 1) * 1024],
                                      in_=x_h[c * TC + i * 128: c * TC + (i + 1) * 128, :]
                                      ).then_inc(m_sx[i % 2], 16)
                    for dg in range(2):
                        kg = i * 2 + dg
                        ptx = (pt0, pt1, pt2, pt3)[kg % 4]
                        prev_tt = ph4_tt.get((i, dg - 2) if dg >= 2 else (i - 1, dg), 0)
                        for s4 in range(4):
                            dt = dg * 4 + s4
                            ptw = (bc_a[28], bc_a[29], bc_a[30], bc_a[31])[kg] \
                                if kg < 4 and s4 == 0 else (m_a, 0)
                            v_t = P(lambda e, d=dt, ii=i, p=ptx, q=s4: e.matmul(
                                        out=p[:, q * 128:(q + 1) * 128],
                                        lhsT=bD.ap()[:, d, ii * 128:(ii + 1) * 128], rhs=s_id[:],
                                        start=(q == 0), stop=(q == 3), is_transpose=True), "tr",
                                    (m_v, prev_tt if s4 == 0 else 0),
                                    (m_v, stt_vals[dt] if kg < 4 else 0),
                                    ptw)
                        v_tt = V(lambda e, d0=dg * 4, ii=i, p=ptx: e.tensor_tensor(
                                     out=bJf[:, ii * 1024 + d0 * 128: ii * 1024 + (d0 + 4) * 128],
                                     in0=p[:],
                                     in1=bBf[:, ii * 1024 + d0 * 128: ii * 1024 + (d0 + 4) * 128],
                                     op=OP.add),
                                 (m_p, v_t), (m_sx[i % 2], v_ld))
                        ph4_tt[(i, dg)] = v_tt
                v_da_last = c_a.n

                for i in range(NT):
                    V(lambda e, ii=i: e.reduce_sum(out=s_stat[:, ii:ii + 1],
                                                   in_=bJf[:, ii * 1024:(ii + 1) * 1024],
                                                   axis=AX.X))
                a_sq2 = {}
                for i in range(NT):
                    a_sq2[i] = A(lambda e, ii=i: e.activation(
                                     out=bA2.ap()[:, ii % 2, :],
                                     in_=bJf[:, ii * 1024:(ii + 1) * 1024], func=AF.Square,
                                     accum_out=s_stat[:, 8 + ii:9 + ii]),
                                 (m_v, ph4_tt[(i, 1)]))
                V(lambda e: e.tensor_scalar(out=s_stat[:, 4:8], in0=s_stat[:, 0:4],
                                            scalar1=1.0 / D, scalar2=None, op0=OP.mult),
                  (m_v, c_v.n))
                V(lambda e: e.tensor_tensor(out=s_stat[:, 0:4], in0=s_stat[:, 4:8],
                                            in1=s_stat[:, 4:8], op=OP.mult), (m_v, c_v.n))
                V(lambda e: e.tensor_scalar(out=s_stat[:, 12:16], in0=s_stat[:, 8:12],
                                            scalar1=1.0 / D, scalar2=None, op0=OP.mult),
                  (m_a, a_sq2[3]))
                v_var2 = V(lambda e: e.tensor_tensor(out=s_stat[:, 8:12], in0=s_stat[:, 12:16],
                                                     in1=s_stat[:, 0:4], op=OP.subtract),
                           (m_v, c_v.n))
                a_sd2 = A(lambda e: e.activation(out=s_stat[:, 12:16], in_=s_stat[:, 8:12],
                                                 func=AF.Sqrt, bias=s_eps[:]), (m_v, v_var2))
                v_rc2 = V(lambda e: e.reciprocal(out=s_stat[:, 0:4], in_=s_stat[:, 12:16]),
                          (m_a, a_sd2))
                for i in range(NT):
                    V(lambda e, ii=i: e.tensor_scalar(
                          out=bCf[:, ii * 1024:(ii + 1) * 1024],
                          in0=bJf[:, ii * 1024:(ii + 1) * 1024],
                          scalar1=s_stat[:, 4 + ii:5 + ii], scalar2=s_stat[:, ii:ii + 1],
                          op0=OP.subtract, op1=OP.mult),
                      (m_v, v_rc2 if i == 0 else 0), (m_a, v_da_last if i == 0 else 0))
                for i in range(NT):
                    V(lambda e, ii=i: e.tensor_reduce(out=s_stat[:, 8 + ii:9 + ii],
                                                      in_=bCf[:, ii * 1024:(ii + 1) * 1024],
                                                      axis=AX.X, op=OP.max,
                                                      apply_absolute_value=True))
                V(lambda e: e.tensor_scalar(out=s_xs[:, 0:4], in0=s_stat[:, 8:12],
                                            scalar1=1e-6, scalar2=1.0 / 127.0,
                                            op0=OP.max, op1=OP.mult), (m_v, c_v.n))
                v_inv = V(lambda e: e.reciprocal(out=s_inv[:, 0:4], in_=s_xs[:, 0:4]),
                          (m_v, c_v.n))
                for i in range(NT):
                    V(lambda e, ii=i: e.tensor_scalar(out=bEf[:, ii * 1024:(ii + 1) * 1024],
                                                      in0=bCf[:, ii * 1024:(ii + 1) * 1024],
                                                      scalar1=s_inv[:, ii:ii + 1], scalar2=128.5,
                                                      op0=OP.mult, op1=OP.add),
                      (m_v, v_inv if i == 0 else 0))
                xqf_h = {}
                for h in range(2):
                    hs = slice(h * 2048, (h + 1) * 2048)
                    V(lambda e, s=hs: e.tensor_scalar(out=bHf[:, s], in0=bEf[:, s],
                                                      scalar1=255.99, scalar2=0.01,
                                                      op0=OP.min, op1=OP.max))
                    src = bHf
                    dsts = [bEf, bFf]
                    for bi, b in enumerate([128.0, 64.0, 32.0, 16.0, 8.0, 4.0, 2.0, 1.0]):
                        dst = dsts[bi % 2]
                        V(lambda e, sc=src, bb=b, s=hs: e.tensor_scalar(
                              out=bBf[:, s], in0=sc[:, s], scalar1=bb, scalar2=bb,
                              op0=OP.is_ge, op1=OP.mult))
                        V(lambda e, sc=src, d=dst, s=hs: e.tensor_tensor(
                              out=d[:, s], in0=sc[:, s], in1=bBf[:, s], op=OP.subtract))
                        src = dst
                    xqf_h[h] = V(lambda e, sc=src, s=hs: e.scalar_tensor_tensor(
                                     out=bBf[:, s], in0=bHf[:, s], scalar=-128.0,
                                     in1=sc[:, s], op0=OP.add, op1=OP.subtract))

                ph6_ev = {}
                e1_v = {}
                e1all_h = {}
                ph6_ev2 = {}

                def emit_xtr(h):
                    for i in (2 * h, 2 * h + 1):
                        for dg in range(2):
                            kg = i * 2 + dg
                            ptx = (pt0, pt1, pt2, pt3)[kg % 4]
                            for s4 in range(4):
                                dt = dg * 4 + s4
                                v_t = P(lambda e, ii=i, d=dt, p=ptx, q=s4: e.matmul(
                                            out=p[:, q * 128:(q + 1) * 128],
                                            lhsT=bBf[:, ii * 1024 + d * 128:
                                                      ii * 1024 + (d + 1) * 128],
                                            rhs=s_id[:], start=(q == 0), stop=(q == 3),
                                            is_transpose=True), "tr",
                                        (m_v, xqf_h[h] if i == 2 * h and dg == 0 and s4 == 0
                                         else 0),
                                        (m_a, ph6_ev.get(kg - 4, 0) if s4 == 0 else 0))
                            v_ev = A(lambda e, ii=i, d0=dg * 4, p=ptx: e.activation(
                                         out=bL.ap()[:, d0:d0 + 4, ii * 128:(ii + 1) * 128],
                                         in_=p.ap().rearrange("p (a b) -> p a b", b=128),
                                         func=AF.Copy), (m_p, v_t))
                            ph6_ev[kg] = v_ev

                def emit_e1mm(h):
                    ts = slice(h * 256, (h + 1) * 256)
                    for oc in range(NOC):
                        psx = psA if oc % 2 == 0 else psB
                        v_mm = 0
                        for kc in range(NDT):
                            v_mm = P(lambda e, o=oc, k=kc, p=psx, s=ts, hh=h: e.matmul(
                                         out=p[:, s], lhsT=s_wq.ap()[:, k, o * 128:(o + 1) * 128],
                                         rhs=bL.ap()[:, k, hh * 256:(hh + 1) * 256],
                                         start=(k == 0), stop=(k == NDT - 1)),
                                     "mm", (m_a, ph6_ev[4 * h + 3] if kc == 0 else 0),
                                     (m_a, v_bce if h == 0 and oc == 0 and kc == 0 else 0),
                                     (m_a, e1_v.get((oc - 2, h), 0) if kc == 0 else 0),
                                     (m_a, e1_v.get((6 + oc % 2, 0), 0)
                                      if h == 1 and kc == 0 else 0))
                        v_e1 = A(lambda e, o=oc, p=psx, s=ts, hh=h: e.activation(
                                     out=bE.ap()[:, o, hh * 256:(hh + 1) * 256], in_=p[:, s],
                                     func=AF.Copy, scale=s_ws[:, o:o + 1]),
                                 (m_p, v_mm),
                                 (m_v, xqf_h[1] if h == 0 and oc >= 4 else 0))
                        e1_v[(oc, h)] = v_e1
                    e1all_h[h] = c_a.n

                def emit_e1t(i):
                    for og in range(2):
                        kg = i * 2 + og
                        ptx = (pt0, pt1, pt2, pt3)[kg % 4]
                        for s4 in range(4):
                            oc = og * 4 + s4
                            v_t = P(lambda e, o=oc, ii=i, p=ptx, q=s4: e.matmul(
                                        out=p[:, q * 128:(q + 1) * 128],
                                        lhsT=bE.ap()[:, o, ii * 128:(ii + 1) * 128],
                                        rhs=s_id[:], start=(q == 0), stop=(q == 3),
                                        is_transpose=True), "tr",
                                    (m_a, e1all_h[i // 2] if og == 0 and s4 == 0 else 0),
                                    (m_a, ph6_ev.get(kg + 4, 0) if i < 2 and s4 == 0 else 0),
                                    (m_a, ph6_ev2.get(kg - 4, 0) if s4 == 0 else 0))
                        v_ev = A(lambda e, o0=og * 4, ii=i, p=ptx: e.activation(
                                     out=bDf[:, ii * 1024 + o0 * 128: ii * 1024 + (o0 + 4) * 128],
                                     in_=p[:], func=AF.Copy), (m_p, v_t))
                        ph6_ev2[kg] = v_ev

                emit_xtr(0)
                emit_e1mm(0)
                emit_xtr(1)
                emit_e1t(0)
                emit_e1t(1)
                emit_e1mm(1)
                emit_e1t(2)
                emit_e1t(3)
                for i in range(NT):
                    g = c * NT + i
                    v_stt = V(lambda e, ii=i: e.scalar_tensor_tensor(
                                  out=bA2.ap()[:, ii % 2, :], in0=bDf[:, ii * 1024:(ii + 1) * 1024],
                                  scalar=s_xs[:, ii:ii + 1], in1=s_biasrep[:],
                                  op0=OP.mult, op1=OP.add),
                              (m_o[g % 2], c_o[g % 2].n if g >= 2 else 0),
                              (m_a, ph6_ev2[i * 2 + 1]))
                    v_t1 = V(lambda e, ii=i: e.tensor_tensor(
                                 out=bA1.ap()[:, ii % 2, :], in0=bA2.ap()[:, ii % 2, :],
                                 in1=bJf[:, ii * 1024:(ii + 1) * 1024], op=OP.add))
                    v_st = c_o[g % 2].inc()
                    if on("gpsimd"):
                        eng.wait_ge(m_v, v_t1)
                        eng.dma_start(out=out_h[c * TC + i * 128: c * TC + (i + 1) * 128, :],
                                      in_=bA1.ap()[:, i % 2, :]).then_inc(m_o[g % 2], 16)
                    if c + 1 < NCH:
                        x0_ld[i] = c_sx[i % 2].inc()
                        if on("sync"):
                            eng.wait_ge(m_v, v_stt)
                            eng.dma_start(out=bDf[:, i * 1024:(i + 1) * 1024],
                                          in_=x_h[(c + 1) * TC + i * 128:
                                                  (c + 1) * TC + (i + 1) * 128, :]
                                          ).then_inc(m_sx[i % 2], 16)
                if on("gpsimd"):
                    wait_all_out(eng)
                barrier()

            if on("gpsimd"):
                wait_all_out(eng)
            barrier()

        with nc.Block() as block:
            @block.sync
            def _(eng):
                emit("sync", eng)

            @block.vector
            def _(eng):
                emit("vector", eng)

            @block.scalar
            def _(eng):
                emit("scalar", eng)

            @block.tensor
            def _(eng):
                emit("tensor", eng)

            @block.gpsimd
            def _(eng):
                emit("gpsimd", eng)

    return nc


_NC_CACHE = {}


def _get_nc():
    if "nc" not in _NC_CACHE:
        _NC_CACHE["nc"] = build_nc()
    return _NC_CACHE["nc"]


def kernel(L0_attn_x, L0_attn_A_log, L0_attn_D, L0_attn_x_proj_w,
           L0_attn_dt_proj_w, L0_attn_dt_proj_b, L0_ffn_weight_q,
           L0_ffn_weight_scale, L0_ffn_bias):
    x = np.asarray(L0_attn_x, dtype=np.float32)
    B = x.shape[0]
    xpw_t = np.ascontiguousarray(np.asarray(L0_attn_x_proj_w, np.float32).T)
    dtw_t = np.ascontiguousarray(np.asarray(L0_attn_dt_proj_w, np.float32).T)
    dtb_dm = np.ascontiguousarray(np.asarray(L0_attn_dt_proj_b, np.float32).reshape(NDT, 128).T)
    alog_dm = np.ascontiguousarray(
        np.asarray(L0_attn_A_log, np.float32).reshape(NDT, 128, NST).transpose(1, 0, 2).reshape(128, 128))
    dvec_dm = np.ascontiguousarray(np.asarray(L0_attn_D, np.float32).reshape(NDT, 128).T)
    wqt = np.ascontiguousarray(
        np.asarray(L0_ffn_weight_q).T.reshape(NDT, 128, D).transpose(1, 0, 2))
    wsc_om = np.ascontiguousarray(np.asarray(L0_ffn_weight_scale, np.float32).reshape(NOC, 128).T)
    biasr = np.ascontiguousarray(
        np.broadcast_to(np.asarray(L0_ffn_bias, np.float32).reshape(1, D), (128, D)))
    ident = np.eye(128, dtype=np.float32)
    onesm = np.zeros((128, 128), dtype=np.float32)
    onesm[0, :] = 1.0
    sel8 = np.zeros((128, 8, 128), dtype=np.float32)
    for r in range(8):
        sel8[r, r, :] = 1.0

    shared = dict(xpw=xpw_t, dtw=dtw_t, dtb=dtb_dm, alog=alog_dm, dvec=dvec_dm,
                  wqt=wqt, wsc=wsc_om, biasr=biasr, ident=ident, onesm=onesm,
                  sel8=sel8)
    in_maps = [dict(x=np.ascontiguousarray(x[b]), **shared) for b in range(B)]

    nc = _get_nc()
    res = run_bass_kernel_spmd(nc, in_maps, core_ids=list(range(B)))
    out = np.stack([np.asarray(r["out"]) for r in res.results], axis=0)
    return out


# version 22
# speedup vs baseline: 1.0123x; 1.0055x over previous
import contextlib

import numpy as np
import ml_dtypes
import concourse.bass as bass
import concourse.mybir as mybir
from concourse.bass_utils import run_bass_kernel_spmd

F32 = mybir.dt.float32
BF16 = mybir.dt.bfloat16
I8 = mybir.dt.int8
AF = mybir.ActivationFunctionType
OP = mybir.AluOpType
AX = mybir.AxisListType

T = 4096
D = 1024
NST = 16
RNK = 64
E = 96
TC = 512
NCH = T // TC
NT = TC // 128
NDT = D // 128
NOC = 8


class Ctr:
    def __init__(self, step=1):
        self.n, self.step = 0, step

    def inc(self):
        self.n += self.step
        return self.n


def build_nc():
    nc = bass.Bass("TRN2", target_bir_lowering=False, detect_race_conditions=False)

    x_h = nc.dram_tensor("x", [T, D], F32, kind="ExternalInput").ap()
    xpw_h = nc.dram_tensor("xpw", [D, E], F32, kind="ExternalInput").ap()
    dtw_h = nc.dram_tensor("dtw", [RNK, D], F32, kind="ExternalInput").ap()
    dtb_h = nc.dram_tensor("dtb", [128, NDT], F32, kind="ExternalInput").ap()
    alog_h = nc.dram_tensor("alog", [128, 128], F32, kind="ExternalInput").ap()
    dvec_h = nc.dram_tensor("dvec", [128, NDT], F32, kind="ExternalInput").ap()
    wqt_h = nc.dram_tensor("wqt", [128, NDT, D], I8, kind="ExternalInput").ap()
    wsc_h = nc.dram_tensor("wsc", [128, NOC], F32, kind="ExternalInput").ap()
    biasr_h = nc.dram_tensor("biasr", [128, D], F32, kind="ExternalInput").ap()
    ident_h = nc.dram_tensor("ident", [128, 128], F32, kind="ExternalInput").ap()
    ones_h = nc.dram_tensor("onesm", [128, 128], F32, kind="ExternalInput").ap()
    sel8_h = nc.dram_tensor("sel8", [128, 8, 128], F32, kind="ExternalInput").ap()
    out_h = nc.dram_tensor("out", [T, D], F32, kind="ExternalOutput").ap()
    bc_d = nc.dram_tensor("bc_dram", [NCH, 2 * NST, TC], F32, kind="Internal").ap()

    st = contextlib.ExitStack()
    sb = lambda name, shape, dt=F32: st.enter_context(nc.sbuf_tensor(name, shape, dt))
    psmk = lambda name, shape, dt=F32: st.enter_context(nc.psum_tensor(name, shape, dt))
    mksem = lambda name: st.enter_context(nc.semaphore(name))

    with st:
        s_xpw = sb("s_xpw", [128, NDT, E], BF16)
        s_dtw = sb("s_dtw", [128, D], BF16)
        s_wq = sb("s_wq", [128, NDT, D], BF16)
        s_A = sb("s_A", [128, 128])
        s_Ae = sb("s_Ae", [128, 128])
        s_dtb = sb("s_dtb", [128, NDT])
        s_D = sb("s_D", [128, NDT])
        s_ws = sb("s_ws", [128, NOC])
        s_biasrep = sb("s_biasrep", [128, D])
        s_id = sb("s_id", [128, 128])
        s_onesm = sb("s_onesm", [128, 128])
        s_carry = sb("s_carry", [128, 2, 128])
        s_stat = sb("s_stat", [128, 16])
        s_xs = sb("s_xs", [128, NT])
        s_inv = sb("s_inv", [128, NT])
        s_row = sb("s_row", [128, 2, TC])
        s_sel8 = sb("s_sel8", [128, 8, 128])
        s_eps = sb("s_eps", [128, 1])
        s_one1 = sb("s_one1", [128, 1])
        s_dA = sb("s_dA", [128, 2, 8, TC])
        s_yq = sb("s_yq", [128, 1, TC])
        s_drank = sb("s_drank", [128, TC], BF16)
        bA1 = sb("bA1", [128, 2, 1024])
        bA2 = sb("bA2", [128, 2, 1024])
        bB = sb("bB", [128, NDT, TC])
        bC = sb("bC", [128, NDT, TC])
        bD = sb("bD", [128, NDT, TC])
        bE = sb("bE", [128, 8, TC])
        bF = sb("bF", [128, 8, TC])
        bH = sb("bH", [128, 8, TC])
        bJ = sb("bJ", [128, 8, TC])
        bL = sb("bL", [128, NDT, TC], BF16)
        psA = psmk("psA", [128, TC])
        psB = psmk("psB", [128, TC])
        psC = psmk("psC", [128, TC])
        pt0 = psmk("pt0", [128, TC])
        pt1 = psmk("pt1", [128, TC])
        pt2 = psmk("pt2", [128, TC])
        pt3 = psmk("pt3", [128, TC])

        bA1f = bA1.ap().rearrange("p a b -> p (a b)")
        bA2f = bA2.ap().rearrange("p a b -> p (a b)")
        bEf = bE.ap().rearrange("p a b -> p (a b)")
        bFf = bF.ap().rearrange("p a b -> p (a b)")
        bHf = bH.ap().rearrange("p a b -> p (a b)")
        bBf = bB.ap().rearrange("p a b -> p (a b)")
        bCf = bC.ap().rearrange("p a b -> p (a b)")
        bDf = bD.ap().rearrange("p a b -> p (a b)")
        bJf = bJ.ap().rearrange("p a b -> p (a b)")

        bar = mksem("bar")
        m_v = mksem("m_v")
        m_a = mksem("m_a")
        m_p = mksem("m_p")
        m_g = mksem("m_g")
        m_o0 = mksem("m_o0")
        m_o1 = mksem("m_o1")
        m_s0 = mksem("m_s0")
        m_s1 = mksem("m_s1")
        m_r0 = mksem("m_r0")
        m_r1 = mksem("m_r1")
        m_q = mksem("m_q")

        def emit(en, eng):
            on = lambda e: en == e
            nbar = [0]
            c_v, c_a, c_p, c_q = Ctr(), Ctr(), Ctr(), Ctr()
            c_g = Ctr(16)
            c_o = [Ctr(16), Ctr(16)]
            m_o = [m_o0, m_o1]
            c_sx = [Ctr(16), Ctr(16)]
            m_sx = [m_s0, m_s1]
            c_r = [Ctr(16), Ctr(16)]
            m_r = [m_r0, m_r1]
            pe_mode = [None]

            def V(fn, *waits):
                val = c_v.inc()
                if on("vector"):
                    for s, v in waits:
                        if v > 0:
                            eng.wait_ge(s, v)
                    fn(eng).then_inc(m_v, 1)
                return val

            def A(fn, *waits):
                val = c_a.inc()
                if on("scalar"):
                    for s, v in waits:
                        if v > 0:
                            eng.wait_ge(s, v)
                    fn(eng).then_inc(m_a, 1)
                return val

            def Q(fn, *waits):
                prev = c_q.n
                val = c_q.inc()
                if on("gpsimd"):
                    for s, v in waits:
                        if v > 0:
                            eng.wait_ge(s, v)
                    if prev > 0:
                        eng.wait_ge(m_q, prev)
                    fn(eng).then_inc(m_q, 1)
                return val

            def P(fn, mode, *waits):
                sw = pe_mode[0] is not None and pe_mode[0] != mode
                pe_mode[0] = mode
                val = c_p.inc()
                if on("tensor"):
                    if sw:
                        eng.drain()
                    for s, v in waits:
                        if v > 0:
                            eng.wait_ge(s, v)
                    fn(eng).then_inc(m_p, 1)
                return val

            def barrier():
                nbar[0] += 1
                if en == "vector":
                    eng.wait_ge(m_v, c_v.n)
                elif en == "scalar":
                    eng.wait_ge(m_a, c_a.n)
                elif en == "tensor":
                    eng.wait_ge(m_p, c_p.n)
                elif en == "sync":
                    eng.wait_ge(m_s0, c_sx[0].n)
                    eng.wait_ge(m_s1, c_sx[1].n)
                    eng.wait_ge(m_r0, c_r[0].n)
                    eng.wait_ge(m_r1, c_r[1].n)
                else:
                    eng.wait_ge(m_g, c_g.n)
                    eng.wait_ge(m_q, c_q.n)
                    eng.wait_ge(m_o0, c_o[0].n)
                    eng.wait_ge(m_o1, c_o[1].n)
                eng.sem_inc(bar, 1)
                eng.wait_ge(bar, 5 * nbar[0])

            def wait_all_out(e):
                e.wait_ge(m_o0, c_o[0].n)
                e.wait_ge(m_o1, c_o[1].n)

            for tgt, src in ((s_xpw[:], xpw_h.rearrange("(c p) e -> p c e", p=128)),
                             (s_dtw[0:RNK, :], dtw_h), (s_wq[:], wqt_h)):
                c_g.inc()
                if on("gpsimd"):
                    eng.dma_start(out=tgt, in_=src).then_inc(m_g, 16)
            for tgt, src in ((s_Ae[:], alog_h), (s_dtb[:], dtb_h), (s_D[:], dvec_h),
                             (s_ws[:], wsc_h), (s_biasrep[:], biasr_h),
                             (s_id[:], ident_h), (s_onesm[:], ones_h),
                             (s_sel8[:], sel8_h)):
                c_sx[0].inc()
                if on("sync"):
                    eng.dma_start(out=tgt, in_=src).then_inc(m_s0, 16)
            if on("gpsimd"):
                eng.wait_ge(m_g, c_g.n)
            if on("sync"):
                eng.wait_ge(m_s0, c_sx[0].n)
            barrier()
            v_aexp = A(lambda e: e.activation(out=s_A[:], in_=s_Ae[:], func=AF.Exp))
            V(lambda e: e.tensor_scalar(out=s_A[:], in0=s_A[:], scalar1=-1.0,
                                        scalar2=None, op0=OP.mult), (m_a, v_aexp))
            V(lambda e: e.memset(s_carry.ap()[:, 0, :], 0.0))
            V(lambda e: e.memset(s_eps[:], 1e-5))
            V(lambda e: e.memset(s_dtw[RNK:128, :], 0.0))
            V(lambda e: e.memset(s_drank[RNK:128, :], 0.0))
            V(lambda e: e.memset(s_row.ap().rearrange("p a b -> p (a b)"), 0.0))
            V(lambda e: e.memset(s_one1[:], 1.0))
            barrier()

            x0_ld = {}
            for c in range(NCH):
                cur, nxt = c % 2, (c + 1) % 2
                if c == 0:
                    for i in range(NT):
                        x0_ld[i] = c_sx[i % 2].inc()
                        if on("sync"):
                            eng.dma_start(out=bDf[:, i * 1024:(i + 1) * 1024],
                                          in_=x_h[i * 128:(i + 1) * 128, :]
                                          ).then_inc(m_sx[i % 2], 16)
                ph1_xln, ph1_evv = {}, {}
                for i in range(NT):
                    V(lambda e, ii=i: e.reduce_sum(out=s_stat[:, ii:ii + 1],
                                                   in_=bDf[:, ii * 1024:(ii + 1) * 1024],
                                                   axis=AX.X), (m_sx[i % 2], x0_ld[i]))
                a_sq = {}
                for i in range(NT):
                    a_sq[i] = A(lambda e, ii=i: e.activation(
                                    out=bCf[:, ii * 1024:(ii + 1) * 1024],
                                    in_=bDf[:, ii * 1024:(ii + 1) * 1024], func=AF.Square,
                                    accum_out=s_stat[:, 8 + ii:9 + ii]),
                                (m_sx[i % 2], x0_ld[i]))
                V(lambda e: e.tensor_scalar(out=s_stat[:, 4:8], in0=s_stat[:, 0:4],
                                            scalar1=1.0 / D, scalar2=None, op0=OP.mult),
                  (m_v, c_v.n))
                V(lambda e: e.tensor_tensor(out=s_stat[:, 0:4], in0=s_stat[:, 4:8],
                                            in1=s_stat[:, 4:8], op=OP.mult), (m_v, c_v.n))
                V(lambda e: e.tensor_scalar(out=s_stat[:, 12:16], in0=s_stat[:, 8:12],
                                            scalar1=1.0 / D, scalar2=None, op0=OP.mult),
                  (m_a, a_sq[3]))
                v_var = V(lambda e: e.tensor_tensor(out=s_stat[:, 8:12], in0=s_stat[:, 12:16],
                                                    in1=s_stat[:, 0:4], op=OP.subtract),
                          (m_v, c_v.n))
                a_sd = A(lambda e: e.activation(out=s_stat[:, 12:16], in_=s_stat[:, 8:12],
                                                func=AF.Sqrt, bias=s_eps[:]), (m_v, v_var))
                v_rc = V(lambda e: e.reciprocal(out=s_stat[:, 0:4], in_=s_stat[:, 12:16]),
                         (m_a, a_sd))
                for i in range(NT):
                    xo = (bA1 if i < 2 else bA2).ap()[:, i % 2, :]
                    ph1_xln[i] = V(lambda e, ii=i, o=xo: e.tensor_scalar(
                                       out=o, in0=bDf[:, ii * 1024:(ii + 1) * 1024],
                                       scalar1=s_stat[:, 4 + ii:5 + ii],
                                       scalar2=s_stat[:, ii:ii + 1],
                                       op0=OP.subtract, op1=OP.mult),
                                   (m_v, v_rc if i == 0 else 0))
                for i in range(NT):
                    for dg in range(2):
                        kg = i * 2 + dg
                        ptx = (pt0, pt1, pt2, pt3)[kg % 4]
                        for s4 in range(4):
                            dt = dg * 4 + s4
                            src_ap = (bA1 if i < 2 else bA2).ap()[:, i % 2, dt * 128:(dt + 1) * 128]
                            v_t = P(lambda e, s=src_ap, p=ptx, q=s4: e.matmul(
                                        out=p[:, q * 128:(q + 1) * 128], lhsT=s, rhs=s_id[:],
                                        start=(q == 0), stop=(q == 3), is_transpose=True), "tr",
                                    (m_v, ph1_xln[i] if s4 == 0 else 0),
                                    (m_a, ph1_evv.get(kg - 4, 0) if s4 == 0 else 0))
                        dst_l = bL.ap()[:, dg * 4:(dg + 1) * 4, i * 128:(i + 1) * 128]
                        a_bl_last = A(lambda e, d=dst_l, p=ptx: e.activation(
                                          out=d, in_=p.ap().rearrange("p (a b) -> p a b", b=128),
                                          func=AF.Copy), (m_p, v_t))
                        dst_b = bB.ap()[:, dg * 4:(dg + 1) * 4, i * 128:(i + 1) * 128]
                        v_ev = A(lambda e, d=dst_b, p=ptx: e.activation(
                                     out=d, in_=p.ap().rearrange("p (a b) -> p a b", b=128),
                                     func=AF.Copy))
                        ph1_evv[kg] = v_ev

                v_xp = 0
                for kc in range(NDT):
                    v_xp = P(lambda e, k=kc: e.matmul(out=psA[0:E, :], lhsT=s_xpw.ap()[:, k, :],
                                                      rhs=bL.ap()[:, k, :], start=(k == 0),
                                                      stop=(k == NDT - 1)), "mm",
                             (m_a, a_bl_last if kc == 0 else 0))
                v_dr = A(lambda e: e.activation(out=s_drank[0:RNK, :], in_=psA[0:RNK, :],
                                                func=AF.Copy), (m_p, v_xp))
                v_bce = A(lambda e: e.activation(out=s_row.ap()[0:32, 0, :],
                                                 in_=psA[RNK:E, :], func=AF.Copy))
                c_g.inc()
                g_bc = c_g.n
                if on("gpsimd"):
                    eng.wait_ge(m_a, v_bce)
                    eng.dma_start(out=bc_d[c], in_=s_row.ap()[0:32, 0, :]).then_inc(m_g, 16)
                bc_tgts = []
                for n in range(NST):
                    bc_tgts.append(bE.ap()[:, n, :] if n < 8 else bJ.ap()[:, n - 8, :])
                for n in range(NST):
                    if n < 8:
                        bc_tgts.append(bF.ap()[:, n, :])
                    elif n < 12:
                        bc_tgts.append(bA1f[:, (n - 8) * TC:(n - 7) * TC])
                    else:
                        bc_tgts.append(bA2f[:, (n - 12) * TC:(n - 11) * TC])
                row_order = list(range(0, 8)) + list(range(16, 24)) + \
                            list(range(8, 16)) + list(range(24, 32))
                a_exp, a_ln = {}, {}
                bc_a, repl_p = {}, {}
                da_pre = {}

                def emit_da(d, oc2, first_wait=0):
                    vals = []
                    for j in range(8):
                        n = oc2 * 8 + j
                        oslot = (d * 2 + oc2) % 2
                        v = A(lambda e, dd=d, nn=n, os=oslot, jj=j: e.activation(
                                  out=s_dA.ap()[:, os, jj, :], in_=bC.ap()[:, dd, :],
                                  func=AF.Exp,
                                  scale=s_A[:, dd * NST + nn: dd * NST + nn + 1]),
                              (m_v, first_wait if j == 0 else 0))
                        vals.append(v)
                    return vals

                def emit_z(dt):
                    psz = psB if dt % 2 == 0 else psC
                    v_z = P(lambda e, d=dt, p=psz: e.matmul(
                                out=p[:], lhsT=s_dtw[:, d * 128:(d + 1) * 128],
                                rhs=s_drank[:], start=True, stop=True), "mm",
                            (m_a, v_dr if dt == 0 else 0), (m_a, a_exp.get(dt - 2, 0)))
                    a_exp[dt] = A(lambda e, d=dt, p=psz: e.activation(
                                      out=bD.ap()[:, d, :], in_=p[:],
                                      func=AF.Exp, bias=s_dtb[:, d:d + 1]), (m_p, v_z))
                    a_ln[dt] = A(lambda e, d=dt: e.activation(out=bC.ap()[:, d, :],
                                                              in_=bD.ap()[:, d, :],
                                                              func=AF.Ln, bias=s_one1[:]))

                grp_val = {}

                def emit_grp_dma(g):
                    base = row_order[8 * g]
                    grp_val[g] = c_r[g % 2].inc()
                    if on("sync"):
                        if g < 2:
                            eng.wait_ge(m_g, g_bc)
                        else:
                            eng.wait_ge(m_p, repl_p[8 * g - 9])
                        eng.dma_start(out=s_row.ap()[0:8, g % 2, :],
                                      in_=bc_d[c][base:base + 8, :]
                                      ).then_inc(m_r[g % 2], 16)

                def emit_repl(g, on_v):
                    for idx in range(8 * g, 8 * g + 8):
                        r = row_order[idx]
                        ptx = (pt0, pt1, pt2, pt3)[idx % 4]
                        pw = bc_a.get(idx - 4, (m_a, 0))
                        repl_p[idx] = P(lambda e, ii=idx, p=ptx: e.matmul(
                                            out=p[:], lhsT=s_sel8.ap()[:, ii % 8, :],
                                            rhs=s_row.ap()[:, (ii // 8) % 2, :],
                                            start=True, stop=True), "mm",
                                        (m_r[g % 2], grp_val[g]), pw,
                                        (m_a, ph1_evv[4 + idx] if idx < 4 else 0))
                        if on_v:
                            bc_a[idx] = (m_v, V(lambda e, rr=r, p=ptx: e.tensor_copy(
                                                    out=bc_tgts[rr], in_=p[:]),
                                                (m_p, repl_p[idx])))
                        else:
                            bc_a[idx] = (m_a, A(lambda e, rr=r, p=ptx: e.activation(
                                                    out=bc_tgts[rr], in_=p[:], func=AF.Copy),
                                                (m_p, repl_p[idx])))

                emit_z(0)
                emit_z(1)
                V(lambda e: e.tensor_tensor(out=bD.ap()[:, 0, :], in0=bC.ap()[:, 0, :],
                                            in1=bB.ap()[:, 0, :], op=OP.mult),
                  (m_a, a_ln[0]))
                emit_grp_dma(0)
                emit_grp_dma(1)
                emit_repl(0, True)
                da_pre[(0, 0)] = emit_da(0, 0)
                emit_z(2)
                emit_z(3)
                emit_repl(1, False)
                da_pre[(0, 1)] = emit_da(0, 1)
                emit_z(4)
                emit_z(5)
                emit_z(6)
                emit_z(7)
                need_c = {0: bc_a[15]}

                k_scan = 0
                scan_v = {}
                stt_vals = {}
                bHr = bH.ap().rearrange("p j t -> p t j")
                bA1q = bA1.ap().rearrange("p a (c f) -> p (a c) f", f=TC)
                bA2q = bA2.ap().rearrange("p a (c f) -> p (a c) f", f=TC)
                for dt in range(NDT):
                    if dt > 0:
                        V(lambda e, d=dt: e.tensor_tensor(out=bD.ap()[:, d, :],
                                                          in0=bC.ap()[:, d, :],
                                                          in1=bB.ap()[:, d, :], op=OP.mult),
                          (m_a, a_ln[dt]))
                    for oc2 in range(2):
                        if dt == 0 and oc2 == 1:
                            emit_grp_dma(2)
                            emit_grp_dma(3)
                            emit_repl(2, False)
                            emit_repl(3, False)
                            need_c[1] = bc_a[31]
                            need_b1 = bc_a[23]
                        goct = dt * 2 + oc2
                        boct = bE.ap() if oc2 == 0 else bJ.ap()
                        for jq in range(8):
                            V(lambda e, d=dt, bo=boct, jj=jq: e.tensor_tensor(
                                  out=bH.ap()[:, jj, :], in0=bD.ap()[:, d, :],
                                  in1=bo[:, jj, :], op=OP.mult),
                              need_b1 if dt == 0 and oc2 == 1 and jq == 0 else (m_a, 0))
                        if (dt, oc2) in da_pre:
                            da_vals = da_pre[(dt, oc2)]
                        else:
                            da_vals = emit_da(dt, oc2, scan_v.get((goct - 2, 7), 0))
                        for j in range(8):
                            n = oc2 * 8 + j
                            oslot = goct % 2
                            v_sc = V(lambda e, d=dt, nn=n, os=oslot, jj=j: e.tensor_tensor_scan(
                                         out=bH.ap()[:, jj, :], data0=s_dA.ap()[:, os, jj, :],
                                         data1=bH.ap()[:, jj, :],
                                         initial=s_carry.ap()[:, cur, d * NST + nn: d * NST + nn + 1],
                                         op0=OP.mult, op1=OP.add),
                                     (m_a, da_vals[7] if j == 0 else 0))
                            scan_v[(goct, j)] = v_sc
                            k_scan += 1
                        cb = dt * NST + oc2 * 8
                        V(lambda e, b=cb: e.tensor_copy(
                              out=s_carry.ap()[:, nxt, b:b + 4],
                              in_=bH.ap()[:, 0:4, TC - 1]))
                        V(lambda e: e.tensor_tensor(out=bH.ap()[:, 0:4, :],
                                                    in0=bH.ap()[:, 0:4, :],
                                                    in1=bF.ap()[:, 0:4, :] if oc2 == 0 else bA1q,
                                                    op=OP.mult),
                          need_c[oc2] if dt == 0 else (m_a, 0))
                        V(lambda e, b=cb: e.tensor_copy(
                              out=s_carry.ap()[:, nxt, b + 4:b + 8],
                              in_=bH.ap()[:, 4:8, TC - 1]))
                        V(lambda e: e.tensor_tensor(out=bH.ap()[:, 4:8, :],
                                                    in0=bH.ap()[:, 4:8, :],
                                                    in1=bF.ap()[:, 4:8, :] if oc2 == 0 else bA2q,
                                                    op=OP.mult))
                        V(lambda e: e.tensor_tensor(out=bH.ap()[:, 0:4, :], in0=bH.ap()[:, 0:4, :],
                                                    in1=bH.ap()[:, 4:8, :], op=OP.add))
                        V(lambda e: e.tensor_tensor(out=bH.ap()[:, 0:2, :], in0=bH.ap()[:, 0:2, :],
                                                    in1=bH.ap()[:, 2:4, :], op=OP.add))
                        if oc2 == 0:
                            V(lambda e: e.tensor_tensor(out=s_yq.ap()[:, 0, :],
                                                        in0=bH.ap()[:, 0, :],
                                                        in1=bH.ap()[:, 1, :], op=OP.add),
                              (m_v, c_v.n))
                        else:
                            V(lambda e: e.tensor_tensor(out=bH.ap()[:, 0, :],
                                                        in0=bH.ap()[:, 0, :],
                                                        in1=bH.ap()[:, 1, :], op=OP.add),
                              (m_v, c_v.n))
                    V(lambda e: e.tensor_tensor(out=bH.ap()[:, 0, :], in0=bH.ap()[:, 0, :],
                                                in1=s_yq.ap()[:, 0, :], op=OP.add), (m_v, c_v.n))
                    stt_v = V(lambda e, d=dt: e.scalar_tensor_tensor(out=bD.ap()[:, d, :],
                                                             in0=bB.ap()[:, d, :],
                                                             scalar=s_D[:, d:d + 1],
                                                             in1=bH.ap()[:, 0, :],
                                                             op0=OP.mult, op1=OP.add),
                              (m_v, c_v.n))
                    stt_vals[dt] = stt_v
                v_ph3b_end = c_v.n

                ph4_tt = {}
                for i in range(NT):
                    v_ld = c_sx[i % 2].inc()
                    if on("sync"):
                        eng.wait_ge(m_v, stt_vals[i * 2 + 1])
                        eng.dma_start(out=bBf[:, i * 1024:(i + 1) * 1024],
                                      in_=x_h[c * TC + i * 128: c * TC + (i + 1) * 128, :]
                                      ).then_inc(m_sx[i % 2], 16)
                    for dg in range(2):
                        kg = i * 2 + dg
                        ptx = (pt0, pt1, pt2, pt3)[kg % 4]
                        prev_tt = ph4_tt.get((i, dg - 2) if dg >= 2 else (i - 1, dg), 0)
                        for s4 in range(4):
                            dt = dg * 4 + s4
                            ptw = (bc_a[28], bc_a[29], bc_a[30], bc_a[31])[kg] \
                                if kg < 4 and s4 == 0 else (m_a, 0)
                            v_t = P(lambda e, d=dt, ii=i, p=ptx, q=s4: e.matmul(
                                        out=p[:, q * 128:(q + 1) * 128],
                                        lhsT=bD.ap()[:, d, ii * 128:(ii + 1) * 128], rhs=s_id[:],
                                        start=(q == 0), stop=(q == 3), is_transpose=True), "tr",
                                    (m_v, prev_tt if s4 == 0 else 0),
                                    (m_v, stt_vals[dt] if kg < 4 else 0),
                                    ptw)
                        v_tt = V(lambda e, d0=dg * 4, ii=i, p=ptx: e.tensor_tensor(
                                     out=bJf[:, ii * 1024 + d0 * 128: ii * 1024 + (d0 + 4) * 128],
                                     in0=p[:],
                                     in1=bBf[:, ii * 1024 + d0 * 128: ii * 1024 + (d0 + 4) * 128],
                                     op=OP.add),
                                 (m_p, v_t), (m_sx[i % 2], v_ld))
                        ph4_tt[(i, dg)] = v_tt
                v_da_last = c_a.n

                for i in range(NT):
                    V(lambda e, ii=i: e.reduce_sum(out=s_stat[:, ii:ii + 1],
                                                   in_=bJf[:, ii * 1024:(ii + 1) * 1024],
                                                   axis=AX.X))
                a_sq2 = {}
                for i in range(NT):
                    a_sq2[i] = A(lambda e, ii=i: e.activation(
                                     out=bA2.ap()[:, ii % 2, :],
                                     in_=bJf[:, ii * 1024:(ii + 1) * 1024], func=AF.Square,
                                     accum_out=s_stat[:, 8 + ii:9 + ii]),
                                 (m_v, ph4_tt[(i, 1)]))
                V(lambda e: e.tensor_scalar(out=s_stat[:, 4:8], in0=s_stat[:, 0:4],
                                            scalar1=1.0 / D, scalar2=None, op0=OP.mult),
                  (m_v, c_v.n))
                V(lambda e: e.tensor_tensor(out=s_stat[:, 0:4], in0=s_stat[:, 4:8],
                                            in1=s_stat[:, 4:8], op=OP.mult), (m_v, c_v.n))
                V(lambda e: e.tensor_scalar(out=s_stat[:, 12:16], in0=s_stat[:, 8:12],
                                            scalar1=1.0 / D, scalar2=None, op0=OP.mult),
                  (m_a, a_sq2[3]))
                v_var2 = V(lambda e: e.tensor_tensor(out=s_stat[:, 8:12], in0=s_stat[:, 12:16],
                                                     in1=s_stat[:, 0:4], op=OP.subtract),
                           (m_v, c_v.n))
                a_sd2 = A(lambda e: e.activation(out=s_stat[:, 12:16], in_=s_stat[:, 8:12],
                                                 func=AF.Sqrt, bias=s_eps[:]), (m_v, v_var2))
                v_rc2 = V(lambda e: e.reciprocal(out=s_stat[:, 0:4], in_=s_stat[:, 12:16]),
                          (m_a, a_sd2))
                for i in range(NT):
                    V(lambda e, ii=i: e.tensor_scalar(
                          out=bCf[:, ii * 1024:(ii + 1) * 1024],
                          in0=bJf[:, ii * 1024:(ii + 1) * 1024],
                          scalar1=s_stat[:, 4 + ii:5 + ii], scalar2=s_stat[:, ii:ii + 1],
                          op0=OP.subtract, op1=OP.mult),
                      (m_v, v_rc2 if i == 0 else 0), (m_a, v_da_last if i == 0 else 0))
                for i in range(NT):
                    V(lambda e, ii=i: e.tensor_reduce(out=s_stat[:, 8 + ii:9 + ii],
                                                      in_=bCf[:, ii * 1024:(ii + 1) * 1024],
                                                      axis=AX.X, op=OP.max,
                                                      apply_absolute_value=True))
                V(lambda e: e.tensor_scalar(out=s_xs[:, 0:4], in0=s_stat[:, 8:12],
                                            scalar1=1e-6, scalar2=1.0 / 127.0,
                                            op0=OP.max, op1=OP.mult), (m_v, c_v.n))
                v_inv = V(lambda e: e.reciprocal(out=s_inv[:, 0:4], in_=s_xs[:, 0:4]),
                          (m_v, c_v.n))
                for i in range(NT):
                    V(lambda e, ii=i: e.tensor_scalar(out=bEf[:, ii * 1024:(ii + 1) * 1024],
                                                      in0=bCf[:, ii * 1024:(ii + 1) * 1024],
                                                      scalar1=s_inv[:, ii:ii + 1], scalar2=128.5,
                                                      op0=OP.mult, op1=OP.add),
                      (m_v, v_inv if i == 0 else 0))
                xqf_h = {}
                for h in range(2):
                    hs = slice(h * 2048, (h + 1) * 2048)
                    V(lambda e, s=hs: e.tensor_scalar(out=bHf[:, s], in0=bEf[:, s],
                                                      scalar1=255.99, scalar2=0.01,
                                                      op0=OP.min, op1=OP.max))
                    src = bHf
                    dsts = [bEf, bFf]
                    for bi, b in enumerate([128.0, 64.0, 32.0, 16.0, 8.0, 4.0, 2.0, 1.0]):
                        dst = dsts[bi % 2]
                        V(lambda e, sc=src, bb=b, s=hs: e.tensor_scalar(
                              out=bBf[:, s], in0=sc[:, s], scalar1=bb, scalar2=bb,
                              op0=OP.is_ge, op1=OP.mult))
                        V(lambda e, sc=src, d=dst, s=hs: e.tensor_tensor(
                              out=d[:, s], in0=sc[:, s], in1=bBf[:, s], op=OP.subtract))
                        src = dst
                    xqf_h[h] = V(lambda e, sc=src, s=hs: e.scalar_tensor_tensor(
                                     out=bBf[:, s], in0=bHf[:, s], scalar=-128.0,
                                     in1=sc[:, s], op0=OP.add, op1=OP.subtract))

                ph6_ev = {}
                e1_v = {}
                e1all_h = {}
                ph6_ev2 = {}

                def emit_xtr(h):
                    for i in (2 * h, 2 * h + 1):
                        for dg in range(2):
                            kg = i * 2 + dg
                            ptx = (pt0, pt1, pt2, pt3)[kg % 4]
                            for s4 in range(4):
                                dt = dg * 4 + s4
                                v_t = P(lambda e, ii=i, d=dt, p=ptx, q=s4: e.matmul(
                                            out=p[:, q * 128:(q + 1) * 128],
                                            lhsT=bBf[:, ii * 1024 + d * 128:
                                                      ii * 1024 + (d + 1) * 128],
                                            rhs=s_id[:], start=(q == 0), stop=(q == 3),
                                            is_transpose=True), "tr",
                                        (m_v, xqf_h[h] if i == 2 * h and dg == 0 and s4 == 0
                                         else 0),
                                        (m_a, ph6_ev2.get(kg - 4, 0) if h == 1 and s4 == 0
                                         else 0))
                            v_ev = A(lambda e, ii=i, d0=dg * 4, p=ptx: e.activation(
                                         out=bL.ap()[:, d0:d0 + 4, ii * 128:(ii + 1) * 128],
                                         in_=p.ap().rearrange("p (a b) -> p a b", b=128),
                                         func=AF.Copy), (m_p, v_t))
                            ph6_ev[kg] = v_ev

                def emit_e1mm(h):
                    ts = slice(h * 256, (h + 1) * 256)
                    for oc in range(NOC):
                        psx = psA if oc % 2 == 0 else psB
                        v_mm = 0
                        for kc in range(NDT):
                            v_mm = P(lambda e, o=oc, k=kc, p=psx, s=ts, hh=h: e.matmul(
                                         out=p[:, s], lhsT=s_wq.ap()[:, k, o * 128:(o + 1) * 128],
                                         rhs=bL.ap()[:, k, hh * 256:(hh + 1) * 256],
                                         start=(k == 0), stop=(k == NDT - 1)),
                                     "mm", (m_a, ph6_ev[4 * h + 3] if kc == 0 else 0),
                                     (m_a, v_bce if h == 0 and oc == 0 and kc == 0 else 0),
                                     (m_a, e1_v.get((oc - 2, h), 0) if kc == 0 else 0),
                                     (m_a, e1_v.get((6 + oc % 2, 0), 0)
                                      if h == 1 and kc == 0 else 0))
                        dst_e1 = (bE if oc < 4 else bF).ap()[:, oc % 4, h * 256:(h + 1) * 256]
                        v_e1 = A(lambda e, d=dst_e1, p=psx, s=ts, o=oc: e.activation(
                                     out=d, in_=p[:, s],
                                     func=AF.Copy, scale=s_ws[:, o:o + 1]),
                                 (m_p, v_mm))
                        e1_v[(oc, h)] = v_e1
                    e1all_h[h] = c_a.n

                def emit_e1t(i):
                    for og in range(2):
                        kg = i * 2 + og
                        ptx = (pt0, pt1, pt2, pt3)[kg % 4]
                        for s4 in range(4):
                            oc = og * 4 + s4
                            src_e1 = (bE if oc < 4 else bF).ap()[:, oc % 4,
                                                                   i * 128:(i + 1) * 128]
                            v_t = P(lambda e, s=src_e1, p=ptx, q=s4: e.matmul(
                                        out=p[:, q * 128:(q + 1) * 128],
                                        lhsT=s,
                                        rhs=s_id[:], start=(q == 0), stop=(q == 3),
                                        is_transpose=True), "tr",
                                    (m_a, e1all_h[i // 2] if og == 0 and s4 == 0 else 0),
                                    (m_a, ph6_ev[kg] if s4 == 0 else 0))
                        v_ev = A(lambda e, o0=og * 4, ii=i, p=ptx: e.activation(
                                     out=bDf[:, ii * 1024 + o0 * 128: ii * 1024 + (o0 + 4) * 128],
                                     in_=p[:], func=AF.Copy), (m_p, v_t))
                        ph6_ev2[kg] = v_ev

                emit_xtr(0)
                emit_e1mm(0)
                emit_e1t(0)
                emit_e1t(1)
                emit_xtr(1)
                emit_e1mm(1)
                emit_e1t(2)
                emit_e1t(3)
                for i in range(NT):
                    g = c * NT + i
                    v_stt = V(lambda e, ii=i: e.scalar_tensor_tensor(
                                  out=bA2.ap()[:, ii % 2, :], in0=bDf[:, ii * 1024:(ii + 1) * 1024],
                                  scalar=s_xs[:, ii:ii + 1], in1=s_biasrep[:],
                                  op0=OP.mult, op1=OP.add),
                              (m_o[g % 2], c_o[g % 2].n if g >= 2 else 0),
                              (m_a, ph6_ev2[i * 2 + 1]))
                    v_t1 = V(lambda e, ii=i: e.tensor_tensor(
                                 out=bA1.ap()[:, ii % 2, :], in0=bA2.ap()[:, ii % 2, :],
                                 in1=bJf[:, ii * 1024:(ii + 1) * 1024], op=OP.add))
                    v_st = c_o[g % 2].inc()
                    if on("gpsimd"):
                        eng.wait_ge(m_v, v_t1)
                        eng.dma_start(out=out_h[c * TC + i * 128: c * TC + (i + 1) * 128, :],
                                      in_=bA1.ap()[:, i % 2, :]).then_inc(m_o[g % 2], 16)
                    if c + 1 < NCH:
                        x0_ld[i] = c_sx[i % 2].inc()
                        if on("sync"):
                            eng.wait_ge(m_v, v_stt)
                            eng.dma_start(out=bDf[:, i * 1024:(i + 1) * 1024],
                                          in_=x_h[(c + 1) * TC + i * 128:
                                                  (c + 1) * TC + (i + 1) * 128, :]
                                          ).then_inc(m_sx[i % 2], 16)
                if on("gpsimd"):
                    wait_all_out(eng)
                barrier()

            if on("gpsimd"):
                wait_all_out(eng)
            barrier()

        with nc.Block() as block:
            @block.sync
            def _(eng):
                emit("sync", eng)

            @block.vector
            def _(eng):
                emit("vector", eng)

            @block.scalar
            def _(eng):
                emit("scalar", eng)

            @block.tensor
            def _(eng):
                emit("tensor", eng)

            @block.gpsimd
            def _(eng):
                emit("gpsimd", eng)

    return nc


_NC_CACHE = {}


def _get_nc():
    if "nc" not in _NC_CACHE:
        _NC_CACHE["nc"] = build_nc()
    return _NC_CACHE["nc"]


def kernel(L0_attn_x, L0_attn_A_log, L0_attn_D, L0_attn_x_proj_w,
           L0_attn_dt_proj_w, L0_attn_dt_proj_b, L0_ffn_weight_q,
           L0_ffn_weight_scale, L0_ffn_bias):
    x = np.asarray(L0_attn_x, dtype=np.float32)
    B = x.shape[0]
    xpw_t = np.ascontiguousarray(np.asarray(L0_attn_x_proj_w, np.float32).T)
    dtw_t = np.ascontiguousarray(np.asarray(L0_attn_dt_proj_w, np.float32).T)
    dtb_dm = np.ascontiguousarray(np.asarray(L0_attn_dt_proj_b, np.float32).reshape(NDT, 128).T)
    alog_dm = np.ascontiguousarray(
        np.asarray(L0_attn_A_log, np.float32).reshape(NDT, 128, NST).transpose(1, 0, 2).reshape(128, 128))
    dvec_dm = np.ascontiguousarray(np.asarray(L0_attn_D, np.float32).reshape(NDT, 128).T)
    wqt = np.ascontiguousarray(
        np.asarray(L0_ffn_weight_q).T.reshape(NDT, 128, D).transpose(1, 0, 2))
    wsc_om = np.ascontiguousarray(np.asarray(L0_ffn_weight_scale, np.float32).reshape(NOC, 128).T)
    biasr = np.ascontiguousarray(
        np.broadcast_to(np.asarray(L0_ffn_bias, np.float32).reshape(1, D), (128, D)))
    ident = np.eye(128, dtype=np.float32)
    onesm = np.zeros((128, 128), dtype=np.float32)
    onesm[0, :] = 1.0
    sel8 = np.zeros((128, 8, 128), dtype=np.float32)
    for r in range(8):
        sel8[r, r, :] = 1.0

    shared = dict(xpw=xpw_t, dtw=dtw_t, dtb=dtb_dm, alog=alog_dm, dvec=dvec_dm,
                  wqt=wqt, wsc=wsc_om, biasr=biasr, ident=ident, onesm=onesm,
                  sel8=sel8)
    in_maps = [dict(x=np.ascontiguousarray(x[b]), **shared) for b in range(B)]

    nc = _get_nc()
    res = run_bass_kernel_spmd(nc, in_maps, core_ids=list(range(B)))
    out = np.stack([np.asarray(r["out"]) for r in res.results], axis=0)
    return out


# version 24 (final)
# speedup vs baseline: 1.0263x; 1.0138x over previous
import contextlib

import numpy as np
import ml_dtypes
import concourse.bass as bass
import concourse.mybir as mybir
from concourse.bass_utils import run_bass_kernel_spmd

F32 = mybir.dt.float32
BF16 = mybir.dt.bfloat16
I8 = mybir.dt.int8
AF = mybir.ActivationFunctionType
OP = mybir.AluOpType
AX = mybir.AxisListType

T = 4096
D = 1024
NST = 16
RNK = 64
E = 96
TC = 512
NCH = T // TC
NT = TC // 128
NDT = D // 128
NOC = 8


class Ctr:
    def __init__(self, step=1):
        self.n, self.step = 0, step

    def inc(self):
        self.n += self.step
        return self.n


def build_nc():
    nc = bass.Bass("TRN2", target_bir_lowering=False, detect_race_conditions=False)

    x_h = nc.dram_tensor("x", [T, D], F32, kind="ExternalInput").ap()
    xpw_h = nc.dram_tensor("xpw", [D, E], F32, kind="ExternalInput").ap()
    dtw_h = nc.dram_tensor("dtw", [RNK, D], F32, kind="ExternalInput").ap()
    dtb_h = nc.dram_tensor("dtb", [128, NDT], F32, kind="ExternalInput").ap()
    alog_h = nc.dram_tensor("alog", [128, 128], F32, kind="ExternalInput").ap()
    dvec_h = nc.dram_tensor("dvec", [128, NDT], F32, kind="ExternalInput").ap()
    wqt_h = nc.dram_tensor("wqt", [128, NDT, D], I8, kind="ExternalInput").ap()
    wsc_h = nc.dram_tensor("wsc", [128, NOC], F32, kind="ExternalInput").ap()
    biasr_h = nc.dram_tensor("biasr", [128, D], F32, kind="ExternalInput").ap()
    ident_h = nc.dram_tensor("ident", [128, 128], F32, kind="ExternalInput").ap()
    ones_h = nc.dram_tensor("onesm", [128, 128], F32, kind="ExternalInput").ap()
    sel8_h = nc.dram_tensor("sel8", [128, 8, 128], F32, kind="ExternalInput").ap()
    out_h = nc.dram_tensor("out", [T, D], F32, kind="ExternalOutput").ap()
    bc_d = nc.dram_tensor("bc_dram", [NCH, 2 * NST, TC], F32, kind="Internal").ap()

    st = contextlib.ExitStack()
    sb = lambda name, shape, dt=F32: st.enter_context(nc.sbuf_tensor(name, shape, dt))
    psmk = lambda name, shape, dt=F32: st.enter_context(nc.psum_tensor(name, shape, dt))
    mksem = lambda name: st.enter_context(nc.semaphore(name))

    with st:
        s_xpw = sb("s_xpw", [128, NDT, E], BF16)
        s_dtw = sb("s_dtw", [128, D], BF16)
        s_wq = sb("s_wq", [128, NDT, D], BF16)
        s_A = sb("s_A", [128, 128])
        s_Ae = sb("s_Ae", [128, 128])
        s_dtb = sb("s_dtb", [128, NDT])
        s_D = sb("s_D", [128, NDT])
        s_ws = sb("s_ws", [128, NOC])
        s_biasrep = sb("s_biasrep", [128, D])
        s_id = sb("s_id", [128, 128])
        s_onesm = sb("s_onesm", [128, 128])
        s_carry = sb("s_carry", [128, 2, 128])
        s_stat = sb("s_stat", [128, 16])
        s_xs = sb("s_xs", [128, NT])
        s_inv = sb("s_inv", [128, NT])
        s_row = sb("s_row", [128, 2, TC])
        s_sel8 = sb("s_sel8", [128, 8, 128])
        s_eps = sb("s_eps", [128, 1])
        s_one1 = sb("s_one1", [128, 1])
        s_dA = sb("s_dA", [128, 2, 8, TC])
        s_yq = sb("s_yq", [128, 1, TC])
        s_drank = sb("s_drank", [128, TC], BF16)
        bA1 = sb("bA1", [128, 2, 1024])
        bA2 = sb("bA2", [128, 2, 1024])
        bB = sb("bB", [128, NDT, TC])
        bC = sb("bC", [128, NDT, TC])
        bD = sb("bD", [128, NDT, TC])
        bE = sb("bE", [128, 8, TC])
        bF = sb("bF", [128, 8, TC])
        bH = sb("bH", [128, 8, TC])
        bJ = sb("bJ", [128, 8, TC])
        bL = sb("bL", [128, NDT, TC], BF16)
        psA = psmk("psA", [128, TC])
        psB = psmk("psB", [128, TC])
        psC = psmk("psC", [128, TC])
        pt0 = psmk("pt0", [128, TC])
        pt1 = psmk("pt1", [128, TC])
        pt2 = psmk("pt2", [128, TC])
        pt3 = psmk("pt3", [128, TC])

        bA1f = bA1.ap().rearrange("p a b -> p (a b)")
        bA2f = bA2.ap().rearrange("p a b -> p (a b)")
        bEf = bE.ap().rearrange("p a b -> p (a b)")
        bFf = bF.ap().rearrange("p a b -> p (a b)")
        bHf = bH.ap().rearrange("p a b -> p (a b)")
        bBf = bB.ap().rearrange("p a b -> p (a b)")
        bCf = bC.ap().rearrange("p a b -> p (a b)")
        bDf = bD.ap().rearrange("p a b -> p (a b)")
        bJf = bJ.ap().rearrange("p a b -> p (a b)")

        bar = mksem("bar")
        m_v = mksem("m_v")
        m_a = mksem("m_a")
        m_p = mksem("m_p")
        m_g = mksem("m_g")
        m_o0 = mksem("m_o0")
        m_o1 = mksem("m_o1")
        m_s0 = mksem("m_s0")
        m_s1 = mksem("m_s1")
        m_r0 = mksem("m_r0")
        m_r1 = mksem("m_r1")
        m_q = mksem("m_q")

        def emit(en, eng):
            on = lambda e: en == e
            nbar = [0]
            c_v, c_a, c_p, c_q = Ctr(), Ctr(), Ctr(), Ctr()
            c_g = Ctr(16)
            c_o = [Ctr(16), Ctr(16)]
            m_o = [m_o0, m_o1]
            c_sx = [Ctr(16), Ctr(16)]
            m_sx = [m_s0, m_s1]
            c_r = [Ctr(16), Ctr(16)]
            m_r = [m_r0, m_r1]
            pe_mode = [None]

            def V(fn, *waits):
                val = c_v.inc()
                if on("vector"):
                    for s, v in waits:
                        if v > 0:
                            eng.wait_ge(s, v)
                    fn(eng).then_inc(m_v, 1)
                return val

            def A(fn, *waits):
                val = c_a.inc()
                if on("scalar"):
                    for s, v in waits:
                        if v > 0:
                            eng.wait_ge(s, v)
                    fn(eng).then_inc(m_a, 1)
                return val

            def Q(fn, *waits):
                prev = c_q.n
                val = c_q.inc()
                if on("gpsimd"):
                    for s, v in waits:
                        if v > 0:
                            eng.wait_ge(s, v)
                    if prev > 0:
                        eng.wait_ge(m_q, prev)
                    fn(eng).then_inc(m_q, 1)
                return val

            def P(fn, mode, *waits):
                sw = pe_mode[0] is not None and pe_mode[0] != mode
                pe_mode[0] = mode
                val = c_p.inc()
                if on("tensor"):
                    if sw:
                        eng.drain()
                    for s, v in waits:
                        if v > 0:
                            eng.wait_ge(s, v)
                    fn(eng).then_inc(m_p, 1)
                return val

            def barrier():
                nbar[0] += 1
                if en == "vector":
                    eng.wait_ge(m_v, c_v.n)
                elif en == "scalar":
                    eng.wait_ge(m_a, c_a.n)
                elif en == "tensor":
                    eng.wait_ge(m_p, c_p.n)
                elif en == "sync":
                    eng.wait_ge(m_s0, c_sx[0].n)
                    eng.wait_ge(m_s1, c_sx[1].n)
                    eng.wait_ge(m_r0, c_r[0].n)
                    eng.wait_ge(m_r1, c_r[1].n)
                else:
                    eng.wait_ge(m_g, c_g.n)
                    eng.wait_ge(m_q, c_q.n)
                    eng.wait_ge(m_o0, c_o[0].n)
                    eng.wait_ge(m_o1, c_o[1].n)
                eng.sem_inc(bar, 1)
                eng.wait_ge(bar, 5 * nbar[0])

            def wait_all_out(e):
                e.wait_ge(m_o0, c_o[0].n)
                e.wait_ge(m_o1, c_o[1].n)

            for tgt, src in ((s_xpw[:], xpw_h.rearrange("(c p) e -> p c e", p=128)),
                             (s_dtw[0:RNK, :], dtw_h), (s_wq[:], wqt_h)):
                c_g.inc()
                if on("gpsimd"):
                    eng.dma_start(out=tgt, in_=src).then_inc(m_g, 16)
            for tgt, src in ((s_Ae[:], alog_h), (s_dtb[:], dtb_h), (s_D[:], dvec_h),
                             (s_ws[:], wsc_h), (s_biasrep[:], biasr_h),
                             (s_id[:], ident_h), (s_onesm[:], ones_h),
                             (s_sel8[:], sel8_h)):
                c_sx[0].inc()
                if on("sync"):
                    eng.dma_start(out=tgt, in_=src).then_inc(m_s0, 16)
            if on("gpsimd"):
                eng.wait_ge(m_g, c_g.n)
            if on("sync"):
                eng.wait_ge(m_s0, c_sx[0].n)
            barrier()
            v_aexp = A(lambda e: e.activation(out=s_A[:], in_=s_Ae[:], func=AF.Exp))
            V(lambda e: e.tensor_scalar(out=s_A[:], in0=s_A[:], scalar1=-1.0,
                                        scalar2=None, op0=OP.mult), (m_a, v_aexp))
            V(lambda e: e.memset(s_carry.ap()[:, 0, :], 0.0))
            V(lambda e: e.memset(s_eps[:], 1e-5))
            V(lambda e: e.memset(s_dtw[RNK:128, :], 0.0))
            V(lambda e: e.memset(s_drank[RNK:128, :], 0.0))
            V(lambda e: e.memset(s_row.ap().rearrange("p a b -> p (a b)"), 0.0))
            V(lambda e: e.memset(s_one1[:], 1.0))
            barrier()

            x0_ld = {}
            for c in range(NCH):
                cur, nxt = c % 2, (c + 1) % 2
                if c == 0:
                    for i in range(NT):
                        x0_ld[i] = c_sx[i % 2].inc()
                        if on("sync"):
                            eng.dma_start(out=bDf[:, i * 1024:(i + 1) * 1024],
                                          in_=x_h[i * 128:(i + 1) * 128, :]
                                          ).then_inc(m_sx[i % 2], 16)
                ph1_xln, ph1_evv = {}, {}
                for i in range(NT):
                    V(lambda e, ii=i: e.reduce_sum(out=s_stat[:, ii:ii + 1],
                                                   in_=bDf[:, ii * 1024:(ii + 1) * 1024],
                                                   axis=AX.X), (m_sx[i % 2], x0_ld[i]))
                a_sq = {}
                for i in range(NT):
                    a_sq[i] = A(lambda e, ii=i: e.activation(
                                    out=bCf[:, ii * 1024:(ii + 1) * 1024],
                                    in_=bDf[:, ii * 1024:(ii + 1) * 1024], func=AF.Square,
                                    accum_out=s_stat[:, 8 + ii:9 + ii]),
                                (m_sx[i % 2], x0_ld[i]))
                V(lambda e: e.tensor_scalar(out=s_stat[:, 4:8], in0=s_stat[:, 0:4],
                                            scalar1=1.0 / D, scalar2=None, op0=OP.mult),
                  (m_v, c_v.n))
                V(lambda e: e.tensor_tensor(out=s_stat[:, 0:4], in0=s_stat[:, 4:8],
                                            in1=s_stat[:, 4:8], op=OP.mult), (m_v, c_v.n))
                V(lambda e: e.tensor_scalar(out=s_stat[:, 12:16], in0=s_stat[:, 8:12],
                                            scalar1=1.0 / D, scalar2=None, op0=OP.mult),
                  (m_a, a_sq[3]))
                v_var = V(lambda e: e.tensor_tensor(out=s_stat[:, 8:12], in0=s_stat[:, 12:16],
                                                    in1=s_stat[:, 0:4], op=OP.subtract),
                          (m_v, c_v.n))
                a_sd = A(lambda e: e.activation(out=s_stat[:, 12:16], in_=s_stat[:, 8:12],
                                                func=AF.Sqrt, bias=s_eps[:]), (m_v, v_var))
                v_rc = V(lambda e: e.reciprocal(out=s_stat[:, 0:4], in_=s_stat[:, 12:16]),
                         (m_a, a_sd))
                for i in range(NT):
                    xo = (bA1 if i < 2 else bA2).ap()[:, i % 2, :]
                    ph1_xln[i] = V(lambda e, ii=i, o=xo: e.tensor_scalar(
                                       out=o, in0=bDf[:, ii * 1024:(ii + 1) * 1024],
                                       scalar1=s_stat[:, 4 + ii:5 + ii],
                                       scalar2=s_stat[:, ii:ii + 1],
                                       op0=OP.subtract, op1=OP.mult),
                                   (m_v, v_rc if i == 0 else 0))
                for i in range(NT):
                    for dg in range(2):
                        kg = i * 2 + dg
                        ptx = (pt0, pt1, pt2, pt3)[kg % 4]
                        for s4 in range(4):
                            dt = dg * 4 + s4
                            src_ap = (bA1 if i < 2 else bA2).ap()[:, i % 2, dt * 128:(dt + 1) * 128]
                            v_t = P(lambda e, s=src_ap, p=ptx, q=s4: e.matmul(
                                        out=p[:, q * 128:(q + 1) * 128], lhsT=s, rhs=s_id[:],
                                        start=(q == 0), stop=(q == 3), is_transpose=True), "tr",
                                    (m_v, ph1_xln[i] if s4 == 0 else 0),
                                    (m_a, ph1_evv.get(kg - 4, 0) if s4 == 0 else 0))
                        dst_l = bL.ap()[:, dg * 4:(dg + 1) * 4, i * 128:(i + 1) * 128]
                        a_bl_last = A(lambda e, d=dst_l, p=ptx: e.activation(
                                          out=d, in_=p.ap().rearrange("p (a b) -> p a b", b=128),
                                          func=AF.Copy), (m_p, v_t))
                        dst_b = bB.ap()[:, dg * 4:(dg + 1) * 4, i * 128:(i + 1) * 128]
                        v_ev = A(lambda e, d=dst_b, p=ptx: e.activation(
                                     out=d, in_=p.ap().rearrange("p (a b) -> p a b", b=128),
                                     func=AF.Copy))
                        ph1_evv[kg] = v_ev

                v_xp = 0
                for kc in range(NDT):
                    v_xp = P(lambda e, k=kc: e.matmul(out=psA[0:E, :], lhsT=s_xpw.ap()[:, k, :],
                                                      rhs=bL.ap()[:, k, :], start=(k == 0),
                                                      stop=(k == NDT - 1)), "mm",
                             (m_a, a_bl_last if kc == 0 else 0))
                v_dr = A(lambda e: e.activation(out=s_drank[0:RNK, :], in_=psA[0:RNK, :],
                                                func=AF.Copy), (m_p, v_xp))
                v_bce = A(lambda e: e.activation(out=s_row.ap()[0:32, 0, :],
                                                 in_=psA[RNK:E, :], func=AF.Copy))
                c_g.inc()
                g_bc = c_g.n
                if on("gpsimd"):
                    eng.wait_ge(m_a, v_bce)
                    eng.dma_start(out=bc_d[c], in_=s_row.ap()[0:32, 0, :]).then_inc(m_g, 16)
                bc_tgts = []
                for n in range(NST):
                    bc_tgts.append(bE.ap()[:, n, :] if n < 8 else bJ.ap()[:, n - 8, :])
                for n in range(NST):
                    if n < 8:
                        bc_tgts.append(bF.ap()[:, n, :])
                    elif n < 12:
                        bc_tgts.append(bA1f[:, (n - 8) * TC:(n - 7) * TC])
                    else:
                        bc_tgts.append(bA2f[:, (n - 12) * TC:(n - 11) * TC])
                row_order = list(range(0, 8)) + list(range(16, 24)) + \
                            list(range(8, 16)) + list(range(24, 32))
                a_exp, a_ln = {}, {}
                bc_a, repl_p = {}, {}
                da_pre = {}

                def emit_da(d, oc2, first_wait=0):
                    vals = []
                    for j in range(8):
                        n = oc2 * 8 + j
                        oslot = (d * 2 + oc2) % 2
                        v = A(lambda e, dd=d, nn=n, os=oslot, jj=j: e.activation(
                                  out=s_dA.ap()[:, os, jj, :], in_=bC.ap()[:, dd, :],
                                  func=AF.Exp,
                                  scale=s_A[:, dd * NST + nn: dd * NST + nn + 1]),
                              (m_v, first_wait if j == 0 else 0))
                        vals.append(v)
                    return vals

                def emit_z(dt):
                    psz = psB if dt % 2 == 0 else psC
                    v_z = P(lambda e, d=dt, p=psz: e.matmul(
                                out=p[:], lhsT=s_dtw[:, d * 128:(d + 1) * 128],
                                rhs=s_drank[:], start=True, stop=True), "mm",
                            (m_a, v_dr if dt == 0 else 0), (m_a, a_exp.get(dt - 2, 0)))
                    a_exp[dt] = A(lambda e, d=dt, p=psz: e.activation(
                                      out=bD.ap()[:, d, :], in_=p[:],
                                      func=AF.Exp, bias=s_dtb[:, d:d + 1]), (m_p, v_z))
                    a_ln[dt] = A(lambda e, d=dt: e.activation(out=bC.ap()[:, d, :],
                                                              in_=bD.ap()[:, d, :],
                                                              func=AF.Ln, bias=s_one1[:]))

                grp_val = {}

                def emit_grp_dma(g):
                    base = row_order[8 * g]
                    grp_val[g] = c_r[g % 2].inc()
                    if on("sync"):
                        if g < 2:
                            eng.wait_ge(m_g, g_bc)
                        else:
                            eng.wait_ge(m_p, repl_p[8 * g - 9])
                        eng.dma_start(out=s_row.ap()[0:8, g % 2, :],
                                      in_=bc_d[c][base:base + 8, :]
                                      ).then_inc(m_r[g % 2], 16)

                def emit_repl(g, on_v):
                    for idx in range(8 * g, 8 * g + 8):
                        r = row_order[idx]
                        ptx = (pt0, pt1, pt2, pt3)[idx % 4]
                        pw = bc_a.get(idx - 4, (m_a, 0))
                        repl_p[idx] = P(lambda e, ii=idx, p=ptx: e.matmul(
                                            out=p[:], lhsT=s_sel8.ap()[:, ii % 8, :],
                                            rhs=s_row.ap()[:, (ii // 8) % 2, :],
                                            start=True, stop=True), "mm",
                                        (m_r[g % 2], grp_val[g]), pw,
                                        (m_a, ph1_evv[4 + idx] if idx < 4 else 0))
                        if on_v:
                            bc_a[idx] = (m_v, V(lambda e, rr=r, p=ptx: e.tensor_copy(
                                                    out=bc_tgts[rr], in_=p[:]),
                                                (m_p, repl_p[idx])))
                        else:
                            bc_a[idx] = (m_a, A(lambda e, rr=r, p=ptx: e.activation(
                                                    out=bc_tgts[rr], in_=p[:], func=AF.Copy),
                                                (m_p, repl_p[idx])))

                emit_z(0)
                emit_z(1)
                V(lambda e: e.tensor_tensor(out=bD.ap()[:, 0, :], in0=bC.ap()[:, 0, :],
                                            in1=bB.ap()[:, 0, :], op=OP.mult),
                  (m_a, a_ln[0]))
                emit_grp_dma(0)
                emit_grp_dma(1)
                emit_repl(0, True)
                da_pre[(0, 0)] = emit_da(0, 0)
                emit_z(2)
                emit_z(3)
                emit_repl(1, False)
                da_pre[(0, 1)] = emit_da(0, 1)
                emit_z(4)
                emit_z(5)
                emit_z(6)
                emit_z(7)
                need_c = {0: bc_a[15]}

                k_scan = 0
                scan_v = {}
                stt_vals = {}
                bHr = bH.ap().rearrange("p j t -> p t j")
                bA1q = bA1.ap().rearrange("p a (c f) -> p (a c) f", f=TC)
                bA2q = bA2.ap().rearrange("p a (c f) -> p (a c) f", f=TC)
                for dt in range(NDT):
                    if dt > 0:
                        V(lambda e, d=dt: e.tensor_tensor(out=bD.ap()[:, d, :],
                                                          in0=bC.ap()[:, d, :],
                                                          in1=bB.ap()[:, d, :], op=OP.mult),
                          (m_a, a_ln[dt]))
                    for oc2 in range(2):
                        if dt == 0 and oc2 == 1:
                            emit_grp_dma(2)
                            emit_grp_dma(3)
                            emit_repl(2, False)
                            emit_repl(3, False)
                            need_c[1] = bc_a[31]
                            need_b1 = bc_a[23]
                        goct = dt * 2 + oc2
                        boct = bE.ap() if oc2 == 0 else bJ.ap()
                        for jq in range(8):
                            V(lambda e, d=dt, bo=boct, jj=jq: e.tensor_tensor(
                                  out=bH.ap()[:, jj, :], in0=bD.ap()[:, d, :],
                                  in1=bo[:, jj, :], op=OP.mult),
                              need_b1 if dt == 0 and oc2 == 1 and jq == 0 else (m_a, 0))
                        if (dt, oc2) in da_pre:
                            da_vals = da_pre[(dt, oc2)]
                        else:
                            da_vals = emit_da(dt, oc2, scan_v.get((goct - 2, 7), 0))
                        for j in range(8):
                            n = oc2 * 8 + j
                            oslot = goct % 2
                            v_sc = V(lambda e, d=dt, nn=n, os=oslot, jj=j: e.tensor_tensor_scan(
                                         out=bH.ap()[:, jj, :], data0=s_dA.ap()[:, os, jj, :],
                                         data1=bH.ap()[:, jj, :],
                                         initial=s_carry.ap()[:, cur, d * NST + nn: d * NST + nn + 1],
                                         op0=OP.mult, op1=OP.add),
                                     (m_a, da_vals[7] if j == 0 else 0))
                            scan_v[(goct, j)] = v_sc
                            k_scan += 1
                        cb = dt * NST + oc2 * 8
                        V(lambda e, b=cb: e.tensor_copy(
                              out=s_carry.ap()[:, nxt, b:b + 4],
                              in_=bH.ap()[:, 0:4, TC - 1]))
                        V(lambda e: e.tensor_tensor(out=bH.ap()[:, 0:4, :],
                                                    in0=bH.ap()[:, 0:4, :],
                                                    in1=bF.ap()[:, 0:4, :] if oc2 == 0 else bA1q,
                                                    op=OP.mult),
                          need_c[oc2] if dt == 0 else (m_a, 0))
                        V(lambda e, b=cb: e.tensor_copy(
                              out=s_carry.ap()[:, nxt, b + 4:b + 8],
                              in_=bH.ap()[:, 4:8, TC - 1]))
                        V(lambda e: e.tensor_tensor(out=bH.ap()[:, 4:8, :],
                                                    in0=bH.ap()[:, 4:8, :],
                                                    in1=bF.ap()[:, 4:8, :] if oc2 == 0 else bA2q,
                                                    op=OP.mult))
                        V(lambda e: e.tensor_tensor(out=bH.ap()[:, 0:4, :], in0=bH.ap()[:, 0:4, :],
                                                    in1=bH.ap()[:, 4:8, :], op=OP.add))
                        V(lambda e: e.tensor_tensor(out=bH.ap()[:, 0:2, :], in0=bH.ap()[:, 0:2, :],
                                                    in1=bH.ap()[:, 2:4, :], op=OP.add))
                        if oc2 == 0:
                            V(lambda e: e.tensor_tensor(out=s_yq.ap()[:, 0, :],
                                                        in0=bH.ap()[:, 0, :],
                                                        in1=bH.ap()[:, 1, :], op=OP.add))
                        else:
                            V(lambda e: e.tensor_tensor(out=bH.ap()[:, 0, :],
                                                        in0=bH.ap()[:, 0, :],
                                                        in1=bH.ap()[:, 1, :], op=OP.add))
                    V(lambda e: e.tensor_tensor(out=bH.ap()[:, 0, :], in0=bH.ap()[:, 0, :],
                                                in1=s_yq.ap()[:, 0, :], op=OP.add))
                    stt_v = V(lambda e, d=dt: e.scalar_tensor_tensor(out=bD.ap()[:, d, :],
                                                             in0=bB.ap()[:, d, :],
                                                             scalar=s_D[:, d:d + 1],
                                                             in1=bH.ap()[:, 0, :],
                                                             op0=OP.mult, op1=OP.add))
                    stt_vals[dt] = stt_v
                v_ph3b_end = c_v.n

                ph4_tt = {}
                for i in range(NT):
                    v_ld = c_sx[i % 2].inc()
                    if on("sync"):
                        eng.wait_ge(m_v, stt_vals[i * 2 + 1])
                        eng.dma_start(out=bBf[:, i * 1024:(i + 1) * 1024],
                                      in_=x_h[c * TC + i * 128: c * TC + (i + 1) * 128, :]
                                      ).then_inc(m_sx[i % 2], 16)
                    for dg in range(2):
                        kg = i * 2 + dg
                        ptx = (pt0, pt1, pt2, pt3)[kg % 4]
                        prev_tt = ph4_tt.get((i, dg - 2) if dg >= 2 else (i - 1, dg), 0)
                        for s4 in range(4):
                            dt = dg * 4 + s4
                            ptw = (bc_a[28], bc_a[29], bc_a[30], bc_a[31])[kg] \
                                if kg < 4 and s4 == 0 else (m_a, 0)
                            v_t = P(lambda e, d=dt, ii=i, p=ptx, q=s4: e.matmul(
                                        out=p[:, q * 128:(q + 1) * 128],
                                        lhsT=bD.ap()[:, d, ii * 128:(ii + 1) * 128], rhs=s_id[:],
                                        start=(q == 0), stop=(q == 3), is_transpose=True), "tr",
                                    (m_v, prev_tt if s4 == 0 else 0),
                                    (m_v, stt_vals[dt] if kg < 4 else 0),
                                    ptw)
                        v_tt = V(lambda e, d0=dg * 4, ii=i, p=ptx: e.tensor_tensor(
                                     out=bJf[:, ii * 1024 + d0 * 128: ii * 1024 + (d0 + 4) * 128],
                                     in0=p[:],
                                     in1=bBf[:, ii * 1024 + d0 * 128: ii * 1024 + (d0 + 4) * 128],
                                     op=OP.add),
                                 (m_p, v_t), (m_sx[i % 2], v_ld))
                        ph4_tt[(i, dg)] = v_tt
                v_da_last = c_a.n

                for i in range(NT):
                    V(lambda e, ii=i: e.reduce_sum(out=s_stat[:, ii:ii + 1],
                                                   in_=bJf[:, ii * 1024:(ii + 1) * 1024],
                                                   axis=AX.X))
                a_sq2 = {}
                for i in range(NT):
                    a_sq2[i] = A(lambda e, ii=i: e.activation(
                                     out=bA2.ap()[:, ii % 2, :],
                                     in_=bJf[:, ii * 1024:(ii + 1) * 1024], func=AF.Square,
                                     accum_out=s_stat[:, 8 + ii:9 + ii]),
                                 (m_v, ph4_tt[(i, 1)]))
                V(lambda e: e.tensor_scalar(out=s_stat[:, 4:8], in0=s_stat[:, 0:4],
                                            scalar1=1.0 / D, scalar2=None, op0=OP.mult),
                  (m_v, c_v.n))
                V(lambda e: e.tensor_tensor(out=s_stat[:, 0:4], in0=s_stat[:, 4:8],
                                            in1=s_stat[:, 4:8], op=OP.mult), (m_v, c_v.n))
                V(lambda e: e.tensor_scalar(out=s_stat[:, 12:16], in0=s_stat[:, 8:12],
                                            scalar1=1.0 / D, scalar2=None, op0=OP.mult),
                  (m_a, a_sq2[3]))
                v_var2 = V(lambda e: e.tensor_tensor(out=s_stat[:, 8:12], in0=s_stat[:, 12:16],
                                                     in1=s_stat[:, 0:4], op=OP.subtract),
                           (m_v, c_v.n))
                a_sd2 = A(lambda e: e.activation(out=s_stat[:, 12:16], in_=s_stat[:, 8:12],
                                                 func=AF.Sqrt, bias=s_eps[:]), (m_v, v_var2))
                v_rc2 = V(lambda e: e.reciprocal(out=s_stat[:, 0:4], in_=s_stat[:, 12:16]),
                          (m_a, a_sd2))
                for i in range(NT):
                    V(lambda e, ii=i: e.tensor_scalar(
                          out=bCf[:, ii * 1024:(ii + 1) * 1024],
                          in0=bJf[:, ii * 1024:(ii + 1) * 1024],
                          scalar1=s_stat[:, 4 + ii:5 + ii], scalar2=s_stat[:, ii:ii + 1],
                          op0=OP.subtract, op1=OP.mult),
                      (m_v, v_rc2 if i == 0 else 0), (m_a, v_da_last if i == 0 else 0))
                for i in range(NT):
                    V(lambda e, ii=i: e.tensor_reduce(out=s_stat[:, 8 + ii:9 + ii],
                                                      in_=bCf[:, ii * 1024:(ii + 1) * 1024],
                                                      axis=AX.X, op=OP.max,
                                                      apply_absolute_value=True))
                V(lambda e: e.tensor_scalar(out=s_xs[:, 0:4], in0=s_stat[:, 8:12],
                                            scalar1=1e-6, scalar2=1.0 / 127.0,
                                            op0=OP.max, op1=OP.mult), (m_v, c_v.n))
                v_inv = V(lambda e: e.reciprocal(out=s_inv[:, 0:4], in_=s_xs[:, 0:4]),
                          (m_v, c_v.n))
                for i in range(NT):
                    V(lambda e, ii=i: e.tensor_scalar(out=bEf[:, ii * 1024:(ii + 1) * 1024],
                                                      in0=bCf[:, ii * 1024:(ii + 1) * 1024],
                                                      scalar1=s_inv[:, ii:ii + 1], scalar2=128.5,
                                                      op0=OP.mult, op1=OP.add),
                      (m_v, v_inv if i == 0 else 0))
                xqf_h = {}
                for h in range(2):
                    hs = slice(h * 2048, (h + 1) * 2048)
                    V(lambda e, s=hs: e.tensor_scalar(out=bHf[:, s], in0=bEf[:, s],
                                                      scalar1=255.99, scalar2=0.01,
                                                      op0=OP.min, op1=OP.max))
                    src = bHf
                    dsts = [bEf, bFf]
                    for bi, b in enumerate([128.0, 64.0, 32.0, 16.0, 8.0, 4.0, 2.0, 1.0]):
                        dst = dsts[bi % 2]
                        V(lambda e, sc=src, bb=b, s=hs: e.tensor_scalar(
                              out=bBf[:, s], in0=sc[:, s], scalar1=bb, scalar2=bb,
                              op0=OP.is_ge, op1=OP.mult))
                        V(lambda e, sc=src, d=dst, s=hs: e.tensor_tensor(
                              out=d[:, s], in0=sc[:, s], in1=bBf[:, s], op=OP.subtract))
                        src = dst
                    xqf_h[h] = V(lambda e, sc=src, s=hs: e.scalar_tensor_tensor(
                                     out=bBf[:, s], in0=bHf[:, s], scalar=-128.0,
                                     in1=sc[:, s], op0=OP.add, op1=OP.subtract))

                ph6_ev = {}
                e1_v = {}
                e1all_h = {}
                ph6_ev2 = {}

                def emit_xtr(h):
                    for i in (2 * h, 2 * h + 1):
                        for dg in range(2):
                            kg = i * 2 + dg
                            ptx = (pt0, pt1, pt2, pt3)[kg % 4]
                            for s4 in range(4):
                                dt = dg * 4 + s4
                                v_t = P(lambda e, ii=i, d=dt, p=ptx, q=s4: e.matmul(
                                            out=p[:, q * 128:(q + 1) * 128],
                                            lhsT=bBf[:, ii * 1024 + d * 128:
                                                      ii * 1024 + (d + 1) * 128],
                                            rhs=s_id[:], start=(q == 0), stop=(q == 3),
                                            is_transpose=True), "tr",
                                        (m_v, xqf_h[h] if i == 2 * h and dg == 0 and s4 == 0
                                         else 0),
                                        (m_a, ph6_ev2.get(kg - 4, 0) if h == 1 and s4 == 0
                                         else 0))
                            v_ev = A(lambda e, ii=i, d0=dg * 4, p=ptx: e.activation(
                                         out=bL.ap()[:, d0:d0 + 4, ii * 128:(ii + 1) * 128],
                                         in_=p.ap().rearrange("p (a b) -> p a b", b=128),
                                         func=AF.Copy), (m_p, v_t))
                            ph6_ev[kg] = v_ev

                def emit_e1mm(h):
                    ts = slice(h * 256, (h + 1) * 256)
                    for oc in range(NOC):
                        psx = psA if oc % 2 == 0 else psB
                        v_mm = 0
                        for kc in range(NDT):
                            v_mm = P(lambda e, o=oc, k=kc, p=psx, s=ts, hh=h: e.matmul(
                                         out=p[:, s], lhsT=s_wq.ap()[:, k, o * 128:(o + 1) * 128],
                                         rhs=bL.ap()[:, k, hh * 256:(hh + 1) * 256],
                                         start=(k == 0), stop=(k == NDT - 1)),
                                     "mm", (m_a, ph6_ev[4 * h + 3] if kc == 0 else 0),
                                     (m_a, v_bce if h == 0 and oc == 0 and kc == 0 else 0),
                                     (m_a, e1_v.get((oc - 2, h), 0) if kc == 0 else 0),
                                     (m_a, e1_v.get((6 + oc % 2, 0), 0)
                                      if h == 1 and kc == 0 else 0))
                        dst_e1 = (bE if oc < 4 else bF).ap()[:, oc % 4, h * 256:(h + 1) * 256]
                        v_e1 = A(lambda e, d=dst_e1, p=psx, s=ts, o=oc: e.activation(
                                     out=d, in_=p[:, s],
                                     func=AF.Copy, scale=s_ws[:, o:o + 1]),
                                 (m_p, v_mm))
                        e1_v[(oc, h)] = v_e1
                    e1all_h[h] = c_a.n

                def emit_e1t(i):
                    for og in range(2):
                        kg = i * 2 + og
                        ptx = (pt0, pt1, pt2, pt3)[kg % 4]
                        for s4 in range(4):
                            oc = og * 4 + s4
                            src_e1 = (bE if oc < 4 else bF).ap()[:, oc % 4,
                                                                   i * 128:(i + 1) * 128]
                            v_t = P(lambda e, s=src_e1, p=ptx, q=s4: e.matmul(
                                        out=p[:, q * 128:(q + 1) * 128],
                                        lhsT=s,
                                        rhs=s_id[:], start=(q == 0), stop=(q == 3),
                                        is_transpose=True), "tr",
                                    (m_a, e1all_h[i // 2] if og == 0 and s4 == 0 else 0),
                                    (m_a, ph6_ev[kg] if s4 == 0 else 0))
                        v_ev = A(lambda e, o0=og * 4, ii=i, p=ptx: e.activation(
                                     out=bDf[:, ii * 1024 + o0 * 128: ii * 1024 + (o0 + 4) * 128],
                                     in_=p[:], func=AF.Copy), (m_p, v_t))
                        ph6_ev2[kg] = v_ev

                emit_xtr(0)
                emit_e1mm(0)
                emit_e1t(0)
                emit_e1t(1)
                emit_xtr(1)
                emit_e1mm(1)
                emit_e1t(2)
                emit_e1t(3)
                for i in range(NT):
                    g = c * NT + i
                    v_stt = V(lambda e, ii=i: e.scalar_tensor_tensor(
                                  out=bA2.ap()[:, ii % 2, :], in0=bDf[:, ii * 1024:(ii + 1) * 1024],
                                  scalar=s_xs[:, ii:ii + 1], in1=s_biasrep[:],
                                  op0=OP.mult, op1=OP.add),
                              (m_o[g % 2], c_o[g % 2].n if g >= 2 else 0),
                              (m_a, ph6_ev2[i * 2 + 1]))
                    v_t1 = V(lambda e, ii=i: e.tensor_tensor(
                                 out=bA1.ap()[:, ii % 2, :], in0=bA2.ap()[:, ii % 2, :],
                                 in1=bJf[:, ii * 1024:(ii + 1) * 1024], op=OP.add))
                    v_st = c_o[g % 2].inc()
                    if on("gpsimd"):
                        eng.wait_ge(m_v, v_t1)
                        eng.dma_start(out=out_h[c * TC + i * 128: c * TC + (i + 1) * 128, :],
                                      in_=bA1.ap()[:, i % 2, :]).then_inc(m_o[g % 2], 16)
                    if c + 1 < NCH:
                        x0_ld[i] = c_sx[i % 2].inc()
                        if on("sync"):
                            eng.wait_ge(m_v, v_stt)
                            eng.dma_start(out=bDf[:, i * 1024:(i + 1) * 1024],
                                          in_=x_h[(c + 1) * TC + i * 128:
                                                  (c + 1) * TC + (i + 1) * 128, :]
                                          ).then_inc(m_sx[i % 2], 16)
                if on("gpsimd"):
                    wait_all_out(eng)
                barrier()

            if on("gpsimd"):
                wait_all_out(eng)
            barrier()

        with nc.Block() as block:
            @block.sync
            def _(eng):
                emit("sync", eng)

            @block.vector
            def _(eng):
                emit("vector", eng)

            @block.scalar
            def _(eng):
                emit("scalar", eng)

            @block.tensor
            def _(eng):
                emit("tensor", eng)

            @block.gpsimd
            def _(eng):
                emit("gpsimd", eng)

    return nc


_NC_CACHE = {}


def _get_nc():
    if "nc" not in _NC_CACHE:
        _NC_CACHE["nc"] = build_nc()
    return _NC_CACHE["nc"]


def kernel(L0_attn_x, L0_attn_A_log, L0_attn_D, L0_attn_x_proj_w,
           L0_attn_dt_proj_w, L0_attn_dt_proj_b, L0_ffn_weight_q,
           L0_ffn_weight_scale, L0_ffn_bias):
    x = np.asarray(L0_attn_x, dtype=np.float32)
    B = x.shape[0]
    xpw_t = np.ascontiguousarray(np.asarray(L0_attn_x_proj_w, np.float32).T)
    dtw_t = np.ascontiguousarray(np.asarray(L0_attn_dt_proj_w, np.float32).T)
    dtb_dm = np.ascontiguousarray(np.asarray(L0_attn_dt_proj_b, np.float32).reshape(NDT, 128).T)
    alog_dm = np.ascontiguousarray(
        np.asarray(L0_attn_A_log, np.float32).reshape(NDT, 128, NST).transpose(1, 0, 2).reshape(128, 128))
    dvec_dm = np.ascontiguousarray(np.asarray(L0_attn_D, np.float32).reshape(NDT, 128).T)
    wqt = np.ascontiguousarray(
        np.asarray(L0_ffn_weight_q).T.reshape(NDT, 128, D).transpose(1, 0, 2))
    wsc_om = np.ascontiguousarray(np.asarray(L0_ffn_weight_scale, np.float32).reshape(NOC, 128).T)
    biasr = np.ascontiguousarray(
        np.broadcast_to(np.asarray(L0_ffn_bias, np.float32).reshape(1, D), (128, D)))
    ident = np.eye(128, dtype=np.float32)
    onesm = np.zeros((128, 128), dtype=np.float32)
    onesm[0, :] = 1.0
    sel8 = np.zeros((128, 8, 128), dtype=np.float32)
    for r in range(8):
        sel8[r, r, :] = 1.0

    shared = dict(xpw=xpw_t, dtw=dtw_t, dtb=dtb_dm, alog=alog_dm, dvec=dvec_dm,
                  wqt=wqt, wsc=wsc_om, biasr=biasr, ident=ident, onesm=onesm,
                  sel8=sel8)
    in_maps = [dict(x=np.ascontiguousarray(x[b]), **shared) for b in range(B)]

    nc = _get_nc()
    res = run_bass_kernel_spmd(nc, in_maps, core_ids=list(range(B)))
    out = np.stack([np.asarray(r["out"]) for r in res.results], axis=0)
    return out
